# version 10
# baseline (speedup 1.0000x reference)
import contextlib
import numpy as np
import concourse.bass as bass
import concourse.mybir as mybir
from concourse.bass_utils import run_bass_kernel_spmd
from concourse.alu_op_type import AluOpType

F32 = mybir.dt.float32
BF16 = mybir.dt.bfloat16
AF = mybir.ActivationFunctionType

S = 2048
D = 4096
C = D // 128
HD = 128
NH = 32
NKV = 8
DKV = NKV * HD
G = S // 512
TS = S // 128
NR = 128
EXP_SHIFT = -8.0
QSCALE = 1.0 / np.sqrt(128.0)


def build_nc(upto=7, dbg=False):
    nc = bass.Bass("TRN2", target_bir_lowering=False)

    x = nc.dram_tensor("x", [S, D], F32, kind="ExternalInput").ap()
    wq = nc.dram_tensor("wq", [D, D], F32, kind="ExternalInput").ap()
    wk = nc.dram_tensor("wk", [D, DKV], F32, kind="ExternalInput").ap()
    wv = nc.dram_tensor("wv", [D, DKV], F32, kind="ExternalInput").ap()
    wo = nc.dram_tensor("wo", [D, D], F32, kind="ExternalInput").ap()
    WT = nc.dram_tensor("WT", [D, D], F32, kind="ExternalInput").ap()
    AT = nc.dram_tensor("AT", [D, NR], F32, kind="ExternalInput").ap()
    BS = nc.dram_tensor("BS", [NR, D], F32, kind="ExternalInput").ap()
    bias = nc.dram_tensor("bias", [1, D], F32, kind="ExternalInput").ap()
    lwr = nc.dram_tensor("lwr", [NR, 1], F32, kind="ExternalInput").ap()
    ident = nc.dram_tensor("ident", [128, 128], F32, kind="ExternalInput").ap()

    y = nc.dram_tensor("y", [S, D], F32, kind="ExternalOutput").ap()

    Qs = nc.dram_tensor("Qs", [NH, 128, S], BF16, kind="Internal").ap()
    Ks = nc.dram_tensor("Ks", [NKV, 128, S], BF16, kind="Internal").ap()
    Vs = nc.dram_tensor("Vs", [TS, 128, DKV], BF16, kind="Internal").ap()
    Os = nc.dram_tensor("Os", [C, 128, S], BF16, kind="Internal").ap()
    XNs = nc.dram_tensor("XNs", [128, C * S], BF16, kind="Internal").ap()
    Ps = nc.dram_tensor("Ps", [S, D], F32, kind="Internal").ap()

    dbg_outs = {}
    if dbg:
        for nm, shp, dt in [("dQs", [NH, 128, S], BF16), ("dKs", [NKV, 128, S], BF16),
                            ("dVs", [TS, 128, DKV], BF16), ("dXN", [128, C * S], BF16),
                            ("dOs", [C, 128, S], BF16), ("dPs", [S, D], F32)]:
            dbg_outs[nm] = nc.dram_tensor(nm, shp, dt, kind="ExternalOutput").ap()

    sLx = [nc.alloc_semaphore(f"sLx{i}") for i in range(2)]
    sWb = [nc.alloc_semaphore(f"sWb{i}") for i in range(2)]
    sLC = nc.alloc_semaphore("sLC")
    sBig = nc.alloc_semaphore("sBig")
    sSt = [nc.alloc_semaphore(f"sSt{i}") for i in range(2)]
    sQd = [nc.alloc_semaphore(f"sQd{i}") for i in range(2)]
    sOd = [nc.alloc_semaphore(f"sOd{i}") for i in range(2)]
    sR = [nc.alloc_semaphore(f"sR{i}") for i in range(2)]
    sYd = [nc.alloc_semaphore(f"sYd{i}") for i in range(2)]
    sXl = [nc.alloc_semaphore(f"sXl{i}") for i in range(4)]
    sV = nc.alloc_semaphore("sV")
    sA = nc.alloc_semaphore("sA")
    sT = nc.alloc_semaphore("sT")
    sP = nc.alloc_semaphore("sP")

    cnt = {"sV": 0, "sA": 0, "sT": 0, "sP": 0, "sLC": 0, "sBig": 0}
    fills = {"sLx": [0, 0], "sWb": [0, 0], "sSt": [0, 0], "sQd": [0, 0],
             "sOd": [0, 0], "sR": [0, 0], "sYd": [0, 0], "sXl": [0, 0, 0, 0]}

    es = contextlib.ExitStack()
    with es:
        wbuf = es.enter_context(nc.sbuf_tensor("wbuf", [128, 2, 32, 256], BF16))
        idb = es.enter_context(nc.sbuf_tensor("idb", [128, 128], BF16))
        ones = es.enter_context(nc.sbuf_tensor("ones", [128, 128], BF16))
        cst = es.enter_context(nc.sbuf_tensor("cst", [128, 4], F32))

        exn = es.enter_context(contextlib.ExitStack())
        xn = exn.enter_context(nc.sbuf_tensor("xn", [128, C * S], BF16))

        with contextlib.ExitStack() as e0:
            xbuf = e0.enter_context(nc.sbuf_tensor("xbuf", [128, 2, D], F32))
            xnt = e0.enter_context(nc.sbuf_tensor("xnt", [128, D], BF16))
            stats = e0.enter_context(nc.sbuf_tensor("stats", [128, 48], F32))
            mv = e0.enter_context(nc.sbuf_tensor("mv", [128, 2], F32))
            std_ = e0.enter_context(nc.sbuf_tensor("std_", [128, 1], F32))
            rstd = e0.enter_context(nc.sbuf_tensor("rstd", [128, 1], F32))
            bneg = e0.enter_context(nc.sbuf_tensor("bneg", [128, 1], F32))
            ps_t = e0.enter_context(nc.psum_tensor("ps_t", [128, 2, 2048], BF16))

            v0, a0, t0 = cnt["sV"], cnt["sA"], cnt["sT"]
            lc0 = cnt["sLC"]
            _sid = nc.enter_named_scope("P0_ln", False)[0]
            with nc.Block() as b:
                @b.gpsimd
                def _(g):
                    g.dma_start(idb[:], ident[:]).then_inc(sLC, 16)
                    g.wait_ge(sLC, lc0 + 16)
                cnt["sLC"] += 16

                @b.sync
                def _(s):
                    for t in range(TS):
                        if t >= 2:
                            s.wait_ge(sA, a0 + (2 if t == 2 else 4 * (t - 2)))
                        s.dma_start(xbuf[:, t % 2, :],
                                    x[t * 128:(t + 1) * 128, :]
                                    ).then_inc(sLx[t % 2], 16)
                    s.wait_ge(sLx[0], 16 * (TS // 2))
                    s.wait_ge(sLx[1], 16 * (TS // 2))
                for t in range(TS):
                    fills["sLx"][t % 2] += 1

                @b.vector
                def _(v):
                    v.memset(ones[:], 1.0)
                    v.memset(cst[:, 0:1], 1e-5)
                    v.memset(cst[:, 1:2], EXP_SHIFT)
                    v.memset(cst[:, 2:3], QSCALE)
                    v.memset(cst[:, 3:4], -1.0).then_inc(sV, 1)
                    for t in range(TS):
                        v.wait_ge(sLx[t % 2], 16 * (t // 2 + 1))
                        last = None
                        for i in range(8):
                            last = v.bn_stats(stats[:, i * 6:(i + 1) * 6],
                                              xbuf[:, t % 2, i * 512:(i + 1) * 512])
                        last.then_inc(sV, 1)
                        v.wait_ge(sV, v0 + 1 + 4 * t + 1)
                        v.bn_aggr(mv[:], stats[:]).then_inc(sV, 1)
                        v.wait_ge(sA, a0 + (1 if t == 0 else 4 * t - 1))
                        if t >= 1:
                            v.wait_ge(sA, a0 + (2 if t == 1 else 4 * (t - 1)))
                        v.reciprocal(rstd[:], std_[:]).then_inc(sV, 1)
                        v.wait_ge(sV, v0 + 1 + 4 * t + 3)
                        v.tensor_scalar(bneg[:], mv[:, 0:1], rstd[:], -1.0,
                                        AluOpType.mult,
                                        AluOpType.mult).then_inc(sV, 1)
                cnt["sV"] += 1 + 4 * TS

                @b.scalar
                def _(sc):
                    for t in range(TS):
                        sc.wait_ge(sV, v0 + 1 + 4 * t + 2)
                        sc.activation(std_[:], mv[:, 1:2], AF.Sqrt,
                                      bias=cst[:, 0:1]).then_inc(sA, 1)
                        sc.wait_ge(sV, v0 + 1 + 4 * t + 4)
                        sc.wait_ge(sLx[t % 2], 16 * (t // 2 + 1))
                        if t >= 1:
                            sc.wait_ge(sT, t0 + 2 * t)
                        sc.activation(xnt[:], xbuf[:, t % 2, :], AF.Identity,
                                      bias=bneg[:], scale=rstd[:]).then_inc(sA, 1)
                        if t >= 1:
                            for half in range(2):
                                sc.wait_ge(sT, t0 + 2 * (t - 1) + half + 1)
                                dst = xn[:].rearrange("p (c s) -> p c s", c=C)[
                                    :, half * 16:(half + 1) * 16,
                                    (t - 1) * 128: t * 128]
                                src = ps_t[:, half, :].rearrange(
                                    "p (c s) -> p c s", c=16)
                                sc.activation(dst, src, AF.Copy, bias=0.0,
                                              scale=1.0).then_inc(sA, 1)
                    for half in range(2):
                        sc.wait_ge(sT, t0 + 2 * (TS - 1) + half + 1)
                        dst = xn[:].rearrange("p (c s) -> p c s", c=C)[
                            :, half * 16:(half + 1) * 16, (TS - 1) * 128: TS * 128]
                        src = ps_t[:, half, :].rearrange("p (c s) -> p c s", c=16)
                        sc.activation(dst, src, AF.Copy, bias=0.0,
                                      scale=1.0).then_inc(sA, 1)
                cnt["sA"] += 4 * TS

                @b.tensor
                def _(t_):
                    t_.wait_ge(sLC, lc0 + 16)
                    for t in range(TS):
                        t_.wait_ge(sA, a0 + (2 if t == 0 else 4 * t))
                        for half in range(2):
                            if t >= 1:
                                t_.wait_ge(sA, a0 + 4 * t + 1 + half)
                            last = None
                            for cc in range(16):
                                c = half * 16 + cc
                                last = t_.transpose(
                                    ps_t[:, half, cc * 128:(cc + 1) * 128],
                                    xnt[:, c * 128:(c + 1) * 128], idb[:])
                            last.then_inc(sT, 1)
                cnt["sT"] += 2 * TS
            nc.leave_named_scope("P0_ln", _sid, False)

        def proj_heads_phase(w_ap, n_blocks, out_s, tag):
            with contextlib.ExitStack() as e1:
                stage = e1.enter_context(
                    nc.sbuf_tensor(f"stg{tag}", [128, 2, 512], BF16))
                ps_k = e1.enter_context(
                    nc.psum_tensor(f"psk{tag}", [128, 2, 512], F32))
                n_groups = n_blocks * 8
                t0, a0 = cnt["sT"], cnt["sA"]
                wfill0 = list(fills["sWb"])
                sfill0 = list(fills["sSt"])

                _sid = nc.enter_named_scope("P12_" + tag, False)[0]
                with nc.Block() as b:
                    @b.gpsimd
                    def _(g):
                        for blk in range(n_blocks):
                            if blk >= 2:
                                g.wait_ge(sT, t0 + 8 * (blk - 1))
                            src = w_ap[:, blk * 256:(blk + 1) * 256].rearrange(
                                "(c p) m -> p c m", p=128)
                            g.dma_start(wbuf[:, blk % 2, :, :],
                                        src).then_inc(sWb[blk % 2], 16)
                        for sl in range(2):
                            nfill = (n_blocks - sl + 1) // 2
                            g.wait_ge(sWb[sl], 16 * (wfill0[sl] + nfill))
                    for blk in range(n_blocks):
                        fills["sWb"][blk % 2] += 1

                    @b.tensor
                    def _(t_):
                        for blk in range(n_blocks):
                            t_.wait_ge(sWb[blk % 2],
                                       16 * (wfill0[blk % 2] + blk // 2 + 1))
                            for hh in range(2):
                                for gg in range(4):
                                    idx = blk * 8 + hh * 4 + gg
                                    if idx >= 2:
                                        t_.wait_ge(sA, a0 + idx - 1)
                                    last = None
                                    for c in range(C):
                                        last = t_.matmul(
                                            ps_k[:, idx % 2, :],
                                            wbuf[:, blk % 2, c,
                                                 hh * 128:(hh + 1) * 128],
                                            xn[:, c * S + gg * 512:
                                               c * S + (gg + 1) * 512],
                                            start=(c == 0), stop=(c == C - 1))
                                    last.then_inc(sT, 1)
                    cnt["sT"] += n_groups

                    @b.scalar
                    def _(sc):
                        for idx in range(n_groups):
                            sc.wait_ge(sT, t0 + idx + 1)
                            if idx >= 2:
                                sc.wait_ge(sSt[idx % 2],
                                           16 * (sfill0[idx % 2] + (idx - 2) // 2 + 1))
                            sc.activation(stage[:, idx % 2, :],
                                          ps_k[:, idx % 2, :], AF.Copy, bias=0.0,
                                          scale=cst[:, 2:3] if tag == "q" else 1.0
                                          ).then_inc(sA, 1)
                    cnt["sA"] += n_groups

                    @b.sync
                    def _(s):
                        for idx in range(n_groups):
                            blk, hh, gg = idx // 8, (idx // 4) % 2, idx % 4
                            head = blk * 2 + hh
                            s.wait_ge(sA, a0 + idx + 1)
                            s.dma_start(out_s[head, :, gg * 512:(gg + 1) * 512],
                                        stage[:, idx % 2, :]
                                        ).then_inc(sSt[idx % 2], 16)
                        for sl in range(2):
                            nfill = (n_groups - sl + 1) // 2
                            s.wait_ge(sSt[sl], 16 * (sfill0[sl] + nfill))
                    for idx in range(n_groups):
                        fills["sSt"][idx % 2] += 1
                nc.leave_named_scope("P12_" + tag, _sid, False)

        if upto >= 2:
            proj_heads_phase(wk, NKV // 2, Ks, "k")
        if upto >= 3:
            proj_heads_phase(wq, NH // 2, Qs, "q")

        if upto >= 4:
            with contextlib.ExitStack() as e3:
                stage = e3.enter_context(nc.sbuf_tensor("stgv", [128, 2, 256], BF16))
                ps_v = e3.enter_context(nc.psum_tensor("psv", [128, 4, 512], F32))
                n_blocks = 4
                n_groups = n_blocks * TS
                t0, a0, big0 = cnt["sT"], cnt["sA"], cnt["sBig"]
                wfill0 = list(fills["sWb"])
                sfill0 = list(fills["sSt"])

                _sid = nc.enter_named_scope("P3_v", False)[0]
                with nc.Block() as b:
                    @b.gpsimd
                    def _(g):
                        for blk in range(n_blocks):
                            if blk >= 2:
                                g.wait_ge(sT, t0 + TS * (blk - 1))
                            src = wv[:, blk * 256:(blk + 1) * 256].rearrange(
                                "(c p) m -> p c m", p=128)
                            g.dma_start(wbuf[:, blk % 2, :, :],
                                        src).then_inc(sWb[blk % 2], 16)
                        for sl in range(2):
                            g.wait_ge(sWb[sl], 16 * (wfill0[sl] + 2))
                    for blk in range(n_blocks):
                        fills["sWb"][blk % 2] += 1

                    @b.tensor
                    def _(t_):
                        for blk in range(n_blocks):
                            t_.wait_ge(sWb[blk % 2],
                                       16 * (wfill0[blk % 2] + blk // 2 + 1))
                            for ts in range(TS):
                                idx = blk * TS + ts
                                if idx >= 4:
                                    t_.wait_ge(sA, a0 + idx - 3)
                                last = None
                                for c in range(C):
                                    last = t_.matmul(
                                        ps_v[:, idx % 4, 0:256],
                                        xn[:, c * S + ts * 128:
                                           c * S + (ts + 1) * 128],
                                        wbuf[:, blk % 2, c, :],
                                        start=(c == 0), stop=(c == C - 1))
                                last.then_inc(sT, 1)
                    cnt["sT"] += n_groups

                    @b.scalar
                    def _(sc):
                        for idx in range(n_groups):
                            sc.wait_ge(sT, t0 + idx + 1)
                            if idx >= 2:
                                sc.wait_ge(sSt[idx % 2],
                                           16 * (sfill0[idx % 2] + (idx - 2) // 2 + 1))
                            sc.activation(stage[:, idx % 2, :],
                                          ps_v[:, idx % 4, 0:256],
                                          AF.Copy, bias=0.0,
                                          scale=1.0).then_inc(sA, 1)
                    cnt["sA"] += n_groups

                    @b.sync
                    def _(s):
                        s.dma_start(XNs[:], xn[:]).then_inc(sBig, 16)
                        for idx in range(n_groups):
                            blk, ts = idx // TS, idx % TS
                            s.wait_ge(sA, a0 + idx + 1)
                            s.dma_start(Vs[ts, :, blk * 256:(blk + 1) * 256],
                                        stage[:, idx % 2, :]
                                        ).then_inc(sSt[idx % 2], 16)
                        for sl in range(2):
                            s.wait_ge(sSt[sl], 16 * (sfill0[sl] + n_groups // 2))
                        s.wait_ge(sBig, big0 + 16)
                    cnt["sBig"] += 16
                    for idx in range(n_groups):
                        fills["sSt"][idx % 2] += 1
                nc.leave_named_scope("P3_v", _sid, False)
        elif dbg:
            dsp = nc.alloc_semaphore("dsp")
            with nc.Block() as b:
                @b.sync
                def _(s):
                    s.dma_start(XNs[:], xn[:]).then_inc(dsp, 16)
                    s.wait_ge(dsp, 16)

        exn.close()

        if upto >= 5:
            with contextlib.ExitStack() as e4:
                K_res = e4.enter_context(nc.sbuf_tensor("K_res", [128, NKV * S], BF16))
                V_res = e4.enter_context(nc.sbuf_tensor("V_res", [128, TS * DKV], BF16))
                qb = e4.enter_context(nc.sbuf_tensor("qb", [128, 2, 512], BF16))
                pT = e4.enter_context(nc.sbuf_tensor("pT", [128, 3, 16 * 512], BF16))
                linv = e4.enter_context(nc.sbuf_tensor("linv", [128, 2, 512], F32))
                ost = e4.enter_context(nc.sbuf_tensor("ost", [128, 2, 512], BF16))
                acc = e4.enter_context(nc.sbuf_tensor("acc", [128, 2, 1024], BF16))
                acc5 = e4.enter_context(nc.sbuf_tensor("acc5", [128, 2, 512], BF16))
                lntmp = e4.enter_context(nc.sbuf_tensor("lntmp", [128, 512], F32))
                ps_sc = e4.enter_context(nc.psum_tensor("ps_sc", [128, 4, 512], F32))
                ps_l = e4.enter_context(nc.psum_tensor("ps_l", [128, 2, 512], F32))
                ps_o = e4.enter_context(nc.psum_tensor("ps_o", [128, 2, 512], F32))
                NIT = NH * G
                t0, a0, v0, big0 = cnt["sT"], cnt["sA"], cnt["sV"], cnt["sBig"]
                p0 = cnt["sP"]
                sV2 = nc.alloc_semaphore("sV2")

                def exp_pos(i, m):
                    if i == 0:
                        return a0 + m + 1
                    return a0 + (10 * i - 1 + m if m < 2 else 10 * i + m + 1)

                def expinv_pos(k):
                    return a0 + (10 * (k + 1) + 2 if k <= NIT - 2 else 10 * NIT)
                qfill0 = list(fills["sQd"])
                ofill0 = list(fills["sOd"])

                _sid = nc.enter_named_scope("P4_attn", False)[0]
                with nc.Block() as b:
                    @b.sync
                    def _(s):
                        s.dma_start(K_res[:].rearrange("p (h s) -> p h s", h=NKV),
                                    Ks.rearrange("h p s -> p h s")).then_inc(sBig, 16)
                        s.dma_start(V_res[:].rearrange("p (j d) -> p j d", j=TS),
                                    Vs.rearrange("j p d -> p j d")).then_inc(sBig, 16)
                        for i in range(NIT):
                            if i >= 2:
                                ip = i - 2
                                s.wait_ge(sT, t0 + (16 if ip == 0 else 17 * ip + 15))
                            h, gg = i // G, i % G
                            s.dma_start(qb[:, i % 2, :],
                                        Qs[h, :, gg * 512:(gg + 1) * 512]
                                        ).then_inc(sQd[i % 2], 16)
                        for sl in range(2):
                            s.wait_ge(sQd[sl], 16 * (qfill0[sl] + NIT // 2))
                        s.wait_ge(sBig, big0 + 32)
                    cnt["sBig"] += 32
                    for i in range(NIT):
                        fills["sQd"][i % 2] += 1

                    def emit_score(t_, i, j):
                        kvh = (i // G) // 4
                        sidx = 16 * i + j
                        if sidx >= 4:
                            c = sidx - 4
                            t_.wait_ge(sA, exp_pos(c // 16, (c % 16) // 2))
                        t_.matmul(ps_sc[:, sidx % 4, :],
                                  K_res[:, kvh * S + j * 128:
                                        kvh * S + (j + 1) * 128],
                                  qb[:, i % 2, :],
                                  start=True, stop=True).then_inc(sT, 1)

                    def emit_lsum(t_, i):
                        t_.wait_ge(sV2, 8 * (i + 1))
                        if i >= 2:
                            t_.wait_ge(sV, v0 + 2 * (i - 1))
                        t_.matmul(ps_l[:, i % 2, :], ones[:], acc5[:, i % 2, :],
                                  start=True, stop=True)

                    def emit_pv(t_, i, j):
                        kvh = (i // G) // 4
                        ins = t_.matmul(
                            ps_o[:, i % 2, :],
                            V_res[:, j * DKV + kvh * 128:
                                  j * DKV + (kvh + 1) * 128],
                            pT[:, i % 3, j * 512:(j + 1) * 512],
                            start=(j == 0), stop=(j == 15))
                        if j == 15:
                            ins.then_inc(sT, 1)

                    @b.tensor
                    def _(t_):
                        t_.wait_ge(sBig, big0 + 32)
                        for i in range(NIT):
                            t_.wait_ge(sQd[i % 2], 16 * (qfill0[i % 2] + i // 2 + 1))
                            for j in range(6):
                                emit_score(t_, i, j)
                            if i >= 1:
                                emit_lsum(t_, i - 1)
                            for j in range(6, 16):
                                emit_score(t_, i, j)
                                if i >= 1:
                                    emit_pv(t_, i - 1, j - 6)
                            if i >= 1:
                                for j in range(10, 16):
                                    emit_pv(t_, i - 1, j)
                        emit_lsum(t_, NIT - 1)
                        for j in range(16):
                            emit_pv(t_, NIT - 1, j)
                    cnt["sT"] += 17 * NIT

                    @b.scalar
                    def _(sc):
                        def ln_expinv(k):
                            if k <= NIT - 2:
                                mark = 23 if k == 0 else 17 * (k + 1) + 6
                            else:
                                mark = 17 * NIT
                            sc.wait_ge(sT, t0 + mark)
                            sc.activation(lntmp[:], ps_l[:, k % 2, :],
                                          AF.Ln).then_inc(sA, 1)
                            sc.wait_ge(sA, expinv_pos(k) - 1)
                            sc.activation(linv[:, k % 2, :], lntmp[:], AF.Exp,
                                          scale=cst[:, 3:4]).then_inc(sA, 1)
                        for i in range(NIT):
                            for m in range(8):
                                if i >= 1 and m == 2:
                                    ln_expinv(i - 1)
                                j = 2 * m
                                sidx = 16 * i + j
                                if i >= 3 and m == 0:
                                    sc.wait_ge(sT, t0 + 17 * i - 18)
                                if i == 0:
                                    sc.wait_ge(sT, t0 + j + 2)
                                elif i == 1:
                                    sc.wait_ge(sT, t0 + 17 + j + 1)
                                else:
                                    sc.wait_ge(sT, t0 + 17 * i + j + 1)
                                sc.activation(pT[:, i % 3, j * 512:(j + 2) * 512],
                                              ps_sc[:, sidx % 4:sidx % 4 + 2, :],
                                              AF.Exp,
                                              bias=cst[:, 1:2]).then_inc(sA, 1)
                        ln_expinv(NIT - 1)
                    cnt["sA"] += 10 * NIT

                    def norm_tt(v, i):
                        v.wait_ge(sA, expinv_pos(i)).then_inc(sV, 1)
                        if i >= 2:
                            v.wait_ge(sOd[i % 2],
                                      16 * (ofill0[i % 2] + (i - 2) // 2 + 1))
                        v.tensor_tensor(ost[:, i % 2, :], ps_o[:, i % 2, :],
                                        linv[:, i % 2, :],
                                        AluOpType.mult).then_inc(sV, 1)

                    @b.vector
                    def _(v):
                        for i in range(NIT):
                            if i >= 2:
                                v.wait_ge(sT, t0 + 17 * (i - 2) + 33)
                            for k in range(1, 8):
                                v.wait_ge(sA, exp_pos(i, k))
                                if k > 1:
                                    v.wait_ge(sV2, 8 * i + k - 1)
                                if k == 1:
                                    v.tensor_tensor(acc[:, i % 2, :],
                                                    pT[:, i % 3, 0:1024],
                                                    pT[:, i % 3, 1024:2048],
                                                    AluOpType.add).then_inc(sV2, 1)
                                else:
                                    v.tensor_tensor(acc[:, i % 2, :],
                                                    acc[:, i % 2, :],
                                                    pT[:, i % 3, k * 1024:(k + 1) * 1024],
                                                    AluOpType.add).then_inc(sV2, 1)
                            v.wait_ge(sV2, 8 * i + 7)
                            v.tensor_tensor(acc5[:, i % 2, :], acc[:, i % 2, 0:512],
                                            acc[:, i % 2, 512:1024],
                                            AluOpType.add).then_inc(sV2, 1)
                            if i >= 1:
                                norm_tt(v, i - 1)
                        norm_tt(v, NIT - 1)
                    cnt["sV"] += 2 * NIT

                    @b.gpsimd
                    def _(g):
                        for i in range(NIT):
                            h, gg = i // G, i % G
                            g.wait_ge(sV, v0 + 2 * (i + 1))
                            g.dma_start(Os[h, :, gg * 512:(gg + 1) * 512],
                                        ost[:, i % 2, :]).then_inc(sOd[i % 2], 16)
                        for sl in range(2):
                            g.wait_ge(sOd[sl], 16 * (ofill0[sl] + NIT // 2))
                    for i in range(NIT):
                        fills["sOd"][i % 2] += 1
                nc.leave_named_scope("P4_attn", _sid, False)

        if upto >= 6:
            with contextlib.ExitStack() as e5:
                o_res = e5.enter_context(nc.sbuf_tensor("o_res", [128, C * S], BF16))
                pstage = e5.enter_context(nc.sbuf_tensor("pstage", [128, 2, 256], F32))
                xsl = e5.enter_context(nc.sbuf_tensor("xsl", [128, 4, 256], F32))
                ps_p = e5.enter_context(nc.psum_tensor("ps_p", [128, 4, 512], F32))
                n_blocks = 16
                n_groups = n_blocks * TS
                t0, v5b, big0 = cnt["sT"], cnt["sV"], cnt["sBig"]
                wfill0 = list(fills["sWb"])
                sfill0 = list(fills["sSt"])
                xfill0 = list(fills["sXl"])

                _sid = nc.enter_named_scope("P5_wo", False)[0]
                with nc.Block() as b:
                    @b.sync
                    def _(s):
                        s.dma_start(o_res[:].rearrange("p (c s) -> p c s", c=C),
                                    Os.rearrange("c p s -> p c s")).then_inc(sBig, 16)
                        for idx in range(n_groups):
                            blk, ts = idx // TS, idx % TS
                            s.wait_ge(sV, v5b + idx + 1)
                            s.dma_start(Ps[ts * 128:(ts + 1) * 128,
                                           blk * 256:(blk + 1) * 256],
                                        pstage[:, idx % 2, :]
                                        ).then_inc(sSt[idx % 2], 16)
                        for sl in range(2):
                            s.wait_ge(sSt[sl], 16 * (sfill0[sl] + n_groups // 2))
                        s.wait_ge(sBig, big0 + 16)
                    cnt["sBig"] += 16
                    for idx in range(n_groups):
                        fills["sSt"][idx % 2] += 1

                    @b.scalar
                    def _(sc):
                        for idx in range(n_groups):
                            blk, ts = idx // TS, idx % TS
                            if idx >= 4:
                                sc.wait_ge(sV, v5b + idx - 3)
                            sc.dma_start(xsl[:, idx % 4, :],
                                         x[ts * 128:(ts + 1) * 128,
                                           blk * 256:(blk + 1) * 256]
                                         ).then_inc(sXl[idx % 4], 16)
                    for idx in range(n_groups):
                        fills["sXl"][idx % 4] += 1

                    @b.gpsimd
                    def _(g):
                        for blk in range(n_blocks):
                            if blk >= 2:
                                g.wait_ge(sT, t0 + TS * (blk - 1))
                            src = wo[:, blk * 256:(blk + 1) * 256].rearrange(
                                "(c p) m -> p c m", p=128)
                            g.dma_start(wbuf[:, blk % 2, :, :],
                                        src).then_inc(sWb[blk % 2], 16)
                        for sl in range(2):
                            g.wait_ge(sWb[sl], 16 * (wfill0[sl] + n_blocks // 2))
                    for blk in range(n_blocks):
                        fills["sWb"][blk % 2] += 1

                    @b.tensor
                    def _(t_):
                        t_.wait_ge(sBig, big0 + 16)
                        for blk in range(n_blocks):
                            t_.wait_ge(sWb[blk % 2],
                                       16 * (wfill0[blk % 2] + blk // 2 + 1))
                            for ts in range(TS):
                                idx = blk * TS + ts
                                if idx >= 4:
                                    t_.wait_ge(sV, v5b + idx - 3)
                                last = None
                                for c in range(C):
                                    last = t_.matmul(
                                        ps_p[:, idx % 4, 0:256],
                                        o_res[:, c * S + ts * 128:
                                              c * S + (ts + 1) * 128],
                                        wbuf[:, blk % 2, c, :],
                                        start=(c == 0), stop=(c == C - 1))
                                last.then_inc(sT, 1)
                    cnt["sT"] += n_groups

                    @b.vector
                    def _(v):
                        for idx in range(n_groups):
                            v.wait_ge(sT, t0 + idx + 1)
                            v.wait_ge(sXl[idx % 4],
                                      16 * (xfill0[idx % 4] + idx // 4 + 1))
                            if idx >= 2:
                                v.wait_ge(sSt[idx % 2],
                                          16 * (sfill0[idx % 2] + (idx - 2) // 2 + 1))
                            v.tensor_tensor(pstage[:, idx % 2, :],
                                            ps_p[:, idx % 4, 0:256],
                                            xsl[:, idx % 4, :],
                                            AluOpType.add).then_inc(sV, 1)
                    cnt["sV"] += n_groups
                nc.leave_named_scope("P5_wo", _sid, False)

        if upto >= 7:
            with contextlib.ExitStack() as e6:
                xn2 = e6.enter_context(nc.sbuf_tensor("xn2", [128, C * S], BF16))
                ATb = e6.enter_context(nc.sbuf_tensor("ATb", [128, C * 128], BF16))
                BSb = e6.enter_context(nc.sbuf_tensor("BSb", [128, D], BF16))
                bi_b = e6.enter_context(nc.sbuf_tensor("bi_b", [1, D], BF16))
                lw_sb = e6.enter_context(nc.sbuf_tensor("lw_sb", [128, 1], F32))
                h_all = e6.enter_context(nc.sbuf_tensor("h_all", [128, S], BF16))
                rsl = e6.enter_context(nc.sbuf_tensor("rsl", [128, 2, 256], F32))
                yst = e6.enter_context(nc.sbuf_tensor("yst", [128, 2, 256], F32))
                ps_f = e6.enter_context(nc.psum_tensor("ps_f", [128, 4, 512], F32))
                n_blocks = 16
                NID = n_blocks * TS
                t0, a0, v0, p0, lc0 = (cnt["sT"], cnt["sA"], cnt["sV"], cnt["sP"],
                                       cnt["sLC"])
                wfill0 = list(fills["sWb"])
                rfill0 = list(fills["sR"])
                yfill0 = list(fills["sYd"])

                _sid = nc.enter_named_scope("P6_ffn", False)[0]

                def wt_load(g, blk):
                    if blk >= 2:
                        g.wait_ge(sT, t0 + 4 + TS * (blk - 1))
                    src = WT[:, blk * 256:(blk + 1) * 256].rearrange(
                        "(c p) m -> p c m", p=128)
                    g.dma_start(wbuf[:, blk % 2, :, :],
                                src).then_inc(sWb[blk % 2], 16)

                with nc.Block() as b:
                    @b.gpsimd
                    def _(g):
                        g.dma_start(xn2[:], XNs[:]).then_inc(sLC, 16)
                        g.dma_start(ATb[:], AT.rearrange("(c p) m -> p c m", p=128)
                                    ).then_inc(sLC, 16)
                        g.dma_start(BSb[:], BS[:],
                                    max_dma_last_dim=8192).then_inc(sLC, 16)
                        g.dma_start(bi_b[:], bias[:],
                                    max_dma_last_dim=8192).then_inc(sLC, 16)
                        g.dma_start(lw_sb[:], lwr[:]).then_inc(sLC, 16)
                        wt_load(g, 0)
                        wt_load(g, 1)
                    cnt["sLC"] += 80
                    for blk in range(n_blocks):
                        fills["sWb"][blk % 2] += 1

                    @b.tensor
                    def _(t_):
                        t_.wait_ge(sLC, lc0 + 80)
                        for gg in range(4):
                            if gg >= 2:
                                t_.wait_ge(sV, v0 + gg - 1)
                            last = None
                            for c in range(C):
                                last = t_.matmul(
                                    ps_f[:, gg % 2, :],
                                    ATb[:, c * 128:(c + 1) * 128],
                                    xn2[:, c * S + gg * 512: c * S + (gg + 1) * 512],
                                    start=(c == 0), stop=(c == C - 1))
                            last.then_inc(sT, 1)
                        t_.wait_ge(sV, v0 + 4)
                        for blk in range(n_blocks):
                            t_.wait_ge(sWb[blk % 2],
                                       16 * (wfill0[blk % 2] + blk // 2 + 1))
                            for ts in range(TS):
                                idx = blk * TS + ts
                                if idx >= 4:
                                    t_.wait_ge(sV, v0 + idx + 1)
                                for c in range(C):
                                    t_.matmul(ps_f[:, idx % 4, 0:256],
                                              xn2[:, c * S + ts * 128:
                                                  c * S + (ts + 1) * 128],
                                              wbuf[:, blk % 2, c, :],
                                              start=(c == 0), stop=False)
                                t_.matmul(ps_f[:, idx % 4, 0:256],
                                          h_all[:, ts * 128:(ts + 1) * 128],
                                          BSb[:, blk * 256:(blk + 1) * 256],
                                          start=False, stop=False)
                                t_.matmul(ps_f[:, idx % 4, 0:256],
                                          ones[0:1, :],
                                          bi_b[:, blk * 256:(blk + 1) * 256],
                                          start=False, stop=True).then_inc(sT, 1)
                    cnt["sT"] += 4 + NID

                    @b.vector
                    def _(v):
                        for gg in range(4):
                            v.wait_ge(sT, t0 + gg + 1)
                            v.tensor_scalar(h_all[:, gg * 512:(gg + 1) * 512],
                                            ps_f[:, gg % 2, :], lw_sb[:], 2.0,
                                            AluOpType.mult,
                                            AluOpType.mult).then_inc(sV, 1)
                        rfill06 = list(fills["sR"])
                        for idx in range(NID):
                            v.wait_ge(sT, t0 + 4 + idx + 1)
                            v.wait_ge(sR[idx % 2],
                                      16 * (rfill06[idx % 2] + idx // 2 + 1))
                            if idx >= 2:
                                v.wait_ge(sYd[idx % 2],
                                          16 * (yfill0[idx % 2] + (idx - 2) // 2 + 1))
                            v.tensor_tensor(yst[:, idx % 2, :],
                                            ps_f[:, idx % 4, 0:256],
                                            rsl[:, idx % 2, :],
                                            AluOpType.add).then_inc(sV, 1)
                    cnt["sV"] += 4 + NID

                    @b.sync
                    def _(s):
                        for idx in range(NID):
                            blk, ts = idx // TS, idx % TS
                            if idx >= 2:
                                s.wait_ge(sV, v0 + 4 + idx - 1)
                            s.dma_start(rsl[:, idx % 2, :],
                                        Ps[ts * 128:(ts + 1) * 128,
                                           blk * 256:(blk + 1) * 256]
                                        ).then_inc(sR[idx % 2], 16)
                        for sl in range(2):
                            s.wait_ge(sR[sl], 16 * (rfill0[sl] + NID // 2))
                    for idx in range(NID):
                        fills["sR"][idx % 2] += 1

                    @b.gpsimd
                    def _(g):
                        for idx in range(NID):
                            if idx >= 11 and (idx - 11) % TS == 0:
                                blk = (idx - 11) // TS + 2
                                if blk < n_blocks:
                                    g.wait_ge(sV, v0 + 4 + idx)
                                    wt_load(g, blk)
                        for sl in range(2):
                            g.wait_ge(sWb[sl], 16 * (wfill0[sl] + n_blocks // 2))
                        g.wait_ge(sLC, lc0 + 80)

                    @b.scalar
                    def _(sc):
                        for idx in range(NID):
                            blk, ts = idx // TS, idx % TS
                            sc.wait_ge(sV, v0 + 4 + idx + 1)
                            sc.dma_start(y[ts * 128:(ts + 1) * 128,
                                           blk * 256:(blk + 1) * 256],
                                         yst[:, idx % 2, :]
                                         ).then_inc(sYd[idx % 2], 16)
                        for sl in range(2):
                            sc.wait_ge(sYd[sl], 16 * (yfill0[sl] + NID // 2))
                    for idx in range(NID):
                        fills["sYd"][idx % 2] += 1
                nc.leave_named_scope("P6_ffn", _sid, False)

    if dbg:
        ddone = nc.alloc_semaphore("ddone")
        pairs = [(dbg_outs["dXN"], XNs)]
        if upto >= 2:
            pairs.append((dbg_outs["dKs"], Ks))
        if upto >= 3:
            pairs.append((dbg_outs["dQs"], Qs))
        if upto >= 4:
            pairs.append((dbg_outs["dVs"], Vs))
        if upto >= 5:
            pairs.append((dbg_outs["dOs"], Os))
        if upto >= 6:
            pairs.append((dbg_outs["dPs"], Ps))
        with nc.Block() as b:
            @b.sync
            def _(s):
                for dst, srcap in pairs:
                    s.dma_start(dst[:], srcap[:]).then_inc(ddone, 16)
                s.wait_ge(ddone, 16 * len(pairs))

    return nc


def kernel(L0_attn_x, L0_attn_w_q, L0_attn_w_k, L0_attn_w_v, L0_attn_w_o,
           L0_ffn_lora_A, L0_ffn_lora_B, L0_ffn_W, L0_ffn_bias,
           L0_ffn_lora_weights):
    xs = np.ascontiguousarray(np.asarray(L0_attn_x, dtype=np.float32))
    wq_ = np.ascontiguousarray(np.asarray(L0_attn_w_q, dtype=np.float32))
    wk_ = np.ascontiguousarray(np.asarray(L0_attn_w_k, dtype=np.float32))
    wv_ = np.ascontiguousarray(np.asarray(L0_attn_w_v, dtype=np.float32))
    wo_ = np.ascontiguousarray(np.asarray(L0_attn_w_o, dtype=np.float32))
    WTm = np.ascontiguousarray(np.asarray(L0_ffn_W, dtype=np.float32).T)
    ATm = np.ascontiguousarray(
        np.asarray(L0_ffn_lora_A, dtype=np.float32).reshape(NR, D).T)
    BSm = np.ascontiguousarray(
        np.asarray(L0_ffn_lora_B, dtype=np.float32).transpose(0, 2, 1).reshape(NR, D))
    bias_ = np.ascontiguousarray(
        np.asarray(L0_ffn_bias, dtype=np.float32).reshape(1, D))
    lwr_ = np.ascontiguousarray(
        np.repeat(np.asarray(L0_ffn_lora_weights, dtype=np.float32), 16
                  ).reshape(NR, 1))
    ident = np.eye(128, dtype=np.float32)

    nc = build_nc()
    common = dict(wq=wq_, wk=wk_, wv=wv_, wo=wo_, WT=WTm, AT=ATm, BS=BSm,
                  bias=bias_, lwr=lwr_, ident=ident)
    in_maps = [dict(x=np.ascontiguousarray(xs[i]), **common) for i in range(8)]
    res = run_bass_kernel_spmd(nc, in_maps, core_ids=list(range(8)))
    out = np.stack([res.results[i]["y"] for i in range(8)], axis=0)
    return out


# version 11
# speedup vs baseline: 1.1021x; 1.1021x over previous
import contextlib
import numpy as np
import concourse.bass as bass
import concourse.mybir as mybir
from concourse.bass_utils import run_bass_kernel_spmd
from concourse.alu_op_type import AluOpType

F32 = mybir.dt.float32
BF16 = mybir.dt.bfloat16
AF = mybir.ActivationFunctionType

S = 2048
D = 4096
C = D // 128
HD = 128
NH = 32
NKV = 8
DKV = NKV * HD
G = S // 512
TS = S // 128
NR = 128
EXP_SHIFT = -8.0
QSCALE = 1.0 / np.sqrt(128.0)


def build_nc(upto=7, dbg=False):
    nc = bass.Bass("TRN2", target_bir_lowering=False)

    x = nc.dram_tensor("x", [S, D], F32, kind="ExternalInput").ap()
    wq = nc.dram_tensor("wq", [D, D], F32, kind="ExternalInput").ap()
    wk = nc.dram_tensor("wk", [D, DKV], F32, kind="ExternalInput").ap()
    wv = nc.dram_tensor("wv", [D, DKV], F32, kind="ExternalInput").ap()
    wo = nc.dram_tensor("wo", [D, D], F32, kind="ExternalInput").ap()
    WT = nc.dram_tensor("WT", [D, D], F32, kind="ExternalInput").ap()
    AT = nc.dram_tensor("AT", [D, NR], F32, kind="ExternalInput").ap()
    BS = nc.dram_tensor("BS", [NR, D], F32, kind="ExternalInput").ap()
    bias = nc.dram_tensor("bias", [1, D], F32, kind="ExternalInput").ap()
    lwr = nc.dram_tensor("lwr", [NR, 1], F32, kind="ExternalInput").ap()
    ident = nc.dram_tensor("ident", [128, 128], F32, kind="ExternalInput").ap()

    y = nc.dram_tensor("y", [S, D], F32, kind="ExternalOutput").ap()

    Qs = nc.dram_tensor("Qs", [NH, 128, S], BF16, kind="Internal").ap()
    Ks = nc.dram_tensor("Ks", [NKV, 128, S], BF16, kind="Internal").ap()
    Vs = nc.dram_tensor("Vs", [TS, 128, DKV], BF16, kind="Internal").ap()
    Os = nc.dram_tensor("Os", [C, 128, S], BF16, kind="Internal").ap()
    XNs = nc.dram_tensor("XNs", [128, C * S], BF16, kind="Internal").ap()
    Ps = nc.dram_tensor("Ps", [S, D], F32, kind="Internal").ap()

    dbg_outs = {}
    if dbg:
        for nm, shp, dt in [("dQs", [NH, 128, S], BF16), ("dKs", [NKV, 128, S], BF16),
                            ("dVs", [TS, 128, DKV], BF16), ("dXN", [128, C * S], BF16),
                            ("dOs", [C, 128, S], BF16), ("dPs", [S, D], F32)]:
            dbg_outs[nm] = nc.dram_tensor(nm, shp, dt, kind="ExternalOutput").ap()

    sLx = [nc.alloc_semaphore(f"sLx{i}") for i in range(2)]
    sWb = [nc.alloc_semaphore(f"sWb{i}") for i in range(2)]
    sLC = nc.alloc_semaphore("sLC")
    sBig = nc.alloc_semaphore("sBig")
    sSt = [nc.alloc_semaphore(f"sSt{i}") for i in range(2)]
    sQd = [nc.alloc_semaphore(f"sQd{i}") for i in range(2)]
    sOd = [nc.alloc_semaphore(f"sOd{i}") for i in range(2)]
    sR = [nc.alloc_semaphore(f"sR{i}") for i in range(2)]
    sYd = [nc.alloc_semaphore(f"sYd{i}") for i in range(2)]
    sXl = [nc.alloc_semaphore(f"sXl{i}") for i in range(4)]
    sV = nc.alloc_semaphore("sV")
    sA = nc.alloc_semaphore("sA")
    sT = nc.alloc_semaphore("sT")
    sP = nc.alloc_semaphore("sP")

    cnt = {"sV": 0, "sA": 0, "sT": 0, "sP": 0, "sLC": 0, "sBig": 0}
    fills = {"sLx": [0, 0], "sWb": [0, 0], "sSt": [0, 0], "sQd": [0, 0],
             "sOd": [0, 0], "sR": [0, 0], "sYd": [0, 0], "sXl": [0, 0, 0, 0]}

    es = contextlib.ExitStack()
    with es:
        wbuf = es.enter_context(nc.sbuf_tensor("wbuf", [128, 2, 32, 256], BF16))
        idb = es.enter_context(nc.sbuf_tensor("idb", [128, 128], BF16))
        ones = es.enter_context(nc.sbuf_tensor("ones", [128, 128], BF16))
        cst = es.enter_context(nc.sbuf_tensor("cst", [128, 4], F32))

        exn = es.enter_context(contextlib.ExitStack())
        xn = exn.enter_context(nc.sbuf_tensor("xn", [128, C * S], BF16))

        with contextlib.ExitStack() as e0:
            xbuf = e0.enter_context(nc.sbuf_tensor("xbuf", [128, 2, D], F32))
            xnt = e0.enter_context(nc.sbuf_tensor("xnt", [128, D], BF16))
            stats = e0.enter_context(nc.sbuf_tensor("stats", [128, 48], F32))
            mv = e0.enter_context(nc.sbuf_tensor("mv", [128, 2], F32))
            std_ = e0.enter_context(nc.sbuf_tensor("std_", [128, 1], F32))
            rstd = e0.enter_context(nc.sbuf_tensor("rstd", [128, 1], F32))
            bneg = e0.enter_context(nc.sbuf_tensor("bneg", [128, 1], F32))
            ps_t = e0.enter_context(nc.psum_tensor("ps_t", [128, 2, 2048], BF16))

            v0, a0, t0 = cnt["sV"], cnt["sA"], cnt["sT"]
            lc0 = cnt["sLC"]
            _sid = nc.enter_named_scope("P0_ln", False)[0]
            with nc.Block() as b:
                @b.gpsimd
                def _(g):
                    g.dma_start(idb[:], ident[:]).then_inc(sLC, 16)
                    g.wait_ge(sLC, lc0 + 16)
                cnt["sLC"] += 16

                @b.sync
                def _(s):
                    for t in range(TS):
                        if t >= 2:
                            s.wait_ge(sA, a0 + (2 if t == 2 else 4 * (t - 2)))
                        s.dma_start(xbuf[:, t % 2, :],
                                    x[t * 128:(t + 1) * 128, :]
                                    ).then_inc(sLx[t % 2], 16)
                    s.wait_ge(sLx[0], 16 * (TS // 2))
                    s.wait_ge(sLx[1], 16 * (TS // 2))
                for t in range(TS):
                    fills["sLx"][t % 2] += 1

                @b.vector
                def _(v):
                    v.memset(ones[:], 1.0)
                    v.memset(cst[:, 0:1], 1e-5)
                    v.memset(cst[:, 1:2], EXP_SHIFT)
                    v.memset(cst[:, 2:3], QSCALE)
                    v.memset(cst[:, 3:4], -1.0).then_inc(sV, 1)
                    for t in range(TS):
                        v.wait_ge(sLx[t % 2], 16 * (t // 2 + 1))
                        last = None
                        for i in range(8):
                            last = v.bn_stats(stats[:, i * 6:(i + 1) * 6],
                                              xbuf[:, t % 2, i * 512:(i + 1) * 512])
                        last.then_inc(sV, 1)
                        v.wait_ge(sV, v0 + 1 + 4 * t + 1)
                        v.bn_aggr(mv[:], stats[:]).then_inc(sV, 1)
                        v.wait_ge(sA, a0 + (1 if t == 0 else 4 * t - 1))
                        if t >= 1:
                            v.wait_ge(sA, a0 + (2 if t == 1 else 4 * (t - 1)))
                        v.reciprocal(rstd[:], std_[:]).then_inc(sV, 1)
                        v.wait_ge(sV, v0 + 1 + 4 * t + 3)
                        v.tensor_scalar(bneg[:], mv[:, 0:1], rstd[:], -1.0,
                                        AluOpType.mult,
                                        AluOpType.mult).then_inc(sV, 1)
                cnt["sV"] += 1 + 4 * TS

                @b.scalar
                def _(sc):
                    for t in range(TS):
                        sc.wait_ge(sV, v0 + 1 + 4 * t + 2)
                        sc.activation(std_[:], mv[:, 1:2], AF.Sqrt,
                                      bias=cst[:, 0:1]).then_inc(sA, 1)
                        sc.wait_ge(sV, v0 + 1 + 4 * t + 4)
                        sc.wait_ge(sLx[t % 2], 16 * (t // 2 + 1))
                        if t >= 1:
                            sc.wait_ge(sT, t0 + 2 * t)
                        sc.activation(xnt[:], xbuf[:, t % 2, :], AF.Identity,
                                      bias=bneg[:], scale=rstd[:]).then_inc(sA, 1)
                        if t >= 1:
                            for half in range(2):
                                sc.wait_ge(sT, t0 + 2 * (t - 1) + half + 1)
                                dst = xn[:].rearrange("p (c s) -> p c s", c=C)[
                                    :, half * 16:(half + 1) * 16,
                                    (t - 1) * 128: t * 128]
                                src = ps_t[:, half, :].rearrange(
                                    "p (c s) -> p c s", c=16)
                                sc.activation(dst, src, AF.Copy, bias=0.0,
                                              scale=1.0).then_inc(sA, 1)
                    for half in range(2):
                        sc.wait_ge(sT, t0 + 2 * (TS - 1) + half + 1)
                        dst = xn[:].rearrange("p (c s) -> p c s", c=C)[
                            :, half * 16:(half + 1) * 16, (TS - 1) * 128: TS * 128]
                        src = ps_t[:, half, :].rearrange("p (c s) -> p c s", c=16)
                        sc.activation(dst, src, AF.Copy, bias=0.0,
                                      scale=1.0).then_inc(sA, 1)
                cnt["sA"] += 4 * TS

                @b.tensor
                def _(t_):
                    t_.wait_ge(sLC, lc0 + 16)
                    for t in range(TS):
                        t_.wait_ge(sA, a0 + (2 if t == 0 else 4 * t))
                        for half in range(2):
                            if t >= 1:
                                t_.wait_ge(sA, a0 + 4 * t + 1 + half)
                            last = None
                            for cc in range(16):
                                c = half * 16 + cc
                                last = t_.transpose(
                                    ps_t[:, half, cc * 128:(cc + 1) * 128],
                                    xnt[:, c * 128:(c + 1) * 128], idb[:])
                            last.then_inc(sT, 1)
                cnt["sT"] += 2 * TS
            nc.leave_named_scope("P0_ln", _sid, False)

        def proj_heads_phase(w_ap, n_blocks, out_s, tag):
            with contextlib.ExitStack() as e1:
                stage = e1.enter_context(
                    nc.sbuf_tensor(f"stg{tag}", [128, 2, 512], BF16))
                ps_k = e1.enter_context(
                    nc.psum_tensor(f"psk{tag}", [128, 2, 512], F32))
                n_groups = n_blocks * 8
                t0, a0 = cnt["sT"], cnt["sA"]
                wfill0 = list(fills["sWb"])
                sfill0 = list(fills["sSt"])

                _sid = nc.enter_named_scope("P12_" + tag, False)[0]
                with nc.Block() as b:
                    @b.gpsimd
                    def _(g):
                        for blk in range(n_blocks):
                            if blk >= 2:
                                g.wait_ge(sT, t0 + 8 * (blk - 1))
                            src = w_ap[:, blk * 256:(blk + 1) * 256].rearrange(
                                "(c p) m -> p c m", p=128)
                            g.dma_start(wbuf[:, blk % 2, :, :],
                                        src).then_inc(sWb[blk % 2], 16)
                        for sl in range(2):
                            nfill = (n_blocks - sl + 1) // 2
                            g.wait_ge(sWb[sl], 16 * (wfill0[sl] + nfill))
                    for blk in range(n_blocks):
                        fills["sWb"][blk % 2] += 1

                    @b.tensor
                    def _(t_):
                        for blk in range(n_blocks):
                            t_.wait_ge(sWb[blk % 2],
                                       16 * (wfill0[blk % 2] + blk // 2 + 1))
                            for hh in range(2):
                                for gg in range(4):
                                    idx = blk * 8 + hh * 4 + gg
                                    if idx >= 2:
                                        t_.wait_ge(sA, a0 + idx - 1)
                                    last = None
                                    for c in range(C):
                                        last = t_.matmul(
                                            ps_k[:, idx % 2, :],
                                            wbuf[:, blk % 2, c,
                                                 hh * 128:(hh + 1) * 128],
                                            xn[:, c * S + gg * 512:
                                               c * S + (gg + 1) * 512],
                                            start=(c == 0), stop=(c == C - 1))
                                    last.then_inc(sT, 1)
                    cnt["sT"] += n_groups

                    @b.scalar
                    def _(sc):
                        for idx in range(n_groups):
                            sc.wait_ge(sT, t0 + idx + 1)
                            if idx >= 2:
                                sc.wait_ge(sSt[idx % 2],
                                           16 * (sfill0[idx % 2] + (idx - 2) // 2 + 1))
                            sc.activation(stage[:, idx % 2, :],
                                          ps_k[:, idx % 2, :], AF.Copy, bias=0.0,
                                          scale=cst[:, 2:3] if tag == "q" else 1.0
                                          ).then_inc(sA, 1)
                    cnt["sA"] += n_groups

                    @b.sync
                    def _(s):
                        for idx in range(n_groups):
                            blk, hh, gg = idx // 8, (idx // 4) % 2, idx % 4
                            head = blk * 2 + hh
                            s.wait_ge(sA, a0 + idx + 1)
                            s.dma_start(out_s[head, :, gg * 512:(gg + 1) * 512],
                                        stage[:, idx % 2, :]
                                        ).then_inc(sSt[idx % 2], 16)
                        for sl in range(2):
                            nfill = (n_groups - sl + 1) // 2
                            s.wait_ge(sSt[sl], 16 * (sfill0[sl] + nfill))
                    for idx in range(n_groups):
                        fills["sSt"][idx % 2] += 1
                nc.leave_named_scope("P12_" + tag, _sid, False)

        if upto >= 2:
            proj_heads_phase(wk, NKV // 2, Ks, "k")
        if upto >= 3:
            proj_heads_phase(wq, NH // 2, Qs, "q")

        if upto >= 4:
            with contextlib.ExitStack() as e3:
                stage = e3.enter_context(nc.sbuf_tensor("stgv", [128, 2, 256], BF16))
                ps_v = e3.enter_context(nc.psum_tensor("psv", [128, 4, 512], F32))
                n_blocks = 4
                n_groups = n_blocks * TS
                t0, a0, big0 = cnt["sT"], cnt["sA"], cnt["sBig"]
                wfill0 = list(fills["sWb"])
                sfill0 = list(fills["sSt"])

                _sid = nc.enter_named_scope("P3_v", False)[0]
                with nc.Block() as b:
                    @b.gpsimd
                    def _(g):
                        for blk in range(n_blocks):
                            if blk >= 2:
                                g.wait_ge(sT, t0 + TS * (blk - 1))
                            src = wv[:, blk * 256:(blk + 1) * 256].rearrange(
                                "(c p) m -> p c m", p=128)
                            g.dma_start(wbuf[:, blk % 2, :, :],
                                        src).then_inc(sWb[blk % 2], 16)
                        for sl in range(2):
                            g.wait_ge(sWb[sl], 16 * (wfill0[sl] + 2))
                    for blk in range(n_blocks):
                        fills["sWb"][blk % 2] += 1

                    @b.tensor
                    def _(t_):
                        for blk in range(n_blocks):
                            t_.wait_ge(sWb[blk % 2],
                                       16 * (wfill0[blk % 2] + blk // 2 + 1))
                            for ts in range(TS):
                                idx = blk * TS + ts
                                if idx >= 4:
                                    t_.wait_ge(sA, a0 + idx - 3)
                                last = None
                                for c in range(C):
                                    last = t_.matmul(
                                        ps_v[:, idx % 4, 0:256],
                                        xn[:, c * S + ts * 128:
                                           c * S + (ts + 1) * 128],
                                        wbuf[:, blk % 2, c, :],
                                        start=(c == 0), stop=(c == C - 1))
                                last.then_inc(sT, 1)
                    cnt["sT"] += n_groups

                    @b.scalar
                    def _(sc):
                        for idx in range(n_groups):
                            sc.wait_ge(sT, t0 + idx + 1)
                            if idx >= 2:
                                sc.wait_ge(sSt[idx % 2],
                                           16 * (sfill0[idx % 2] + (idx - 2) // 2 + 1))
                            sc.activation(stage[:, idx % 2, :],
                                          ps_v[:, idx % 4, 0:256],
                                          AF.Copy, bias=0.0,
                                          scale=1.0).then_inc(sA, 1)
                    cnt["sA"] += n_groups

                    @b.sync
                    def _(s):
                        s.dma_start(XNs[:], xn[:]).then_inc(sBig, 16)
                        for idx in range(n_groups):
                            blk, ts = idx // TS, idx % TS
                            s.wait_ge(sA, a0 + idx + 1)
                            s.dma_start(Vs[ts, :, blk * 256:(blk + 1) * 256],
                                        stage[:, idx % 2, :]
                                        ).then_inc(sSt[idx % 2], 16)
                        for sl in range(2):
                            s.wait_ge(sSt[sl], 16 * (sfill0[sl] + n_groups // 2))
                        s.wait_ge(sBig, big0 + 16)
                    cnt["sBig"] += 16
                    for idx in range(n_groups):
                        fills["sSt"][idx % 2] += 1
                nc.leave_named_scope("P3_v", _sid, False)
        elif dbg:
            dsp = nc.alloc_semaphore("dsp")
            with nc.Block() as b:
                @b.sync
                def _(s):
                    s.dma_start(XNs[:], xn[:]).then_inc(dsp, 16)
                    s.wait_ge(dsp, 16)

        exn.close()

        if upto >= 5:
            with contextlib.ExitStack() as e4:
                K_res = e4.enter_context(nc.sbuf_tensor("K_res", [128, NKV * S], BF16))
                V_res = e4.enter_context(nc.sbuf_tensor("V_res", [128, TS * DKV], BF16))
                qb = e4.enter_context(nc.sbuf_tensor("qb", [128, 2, 512], BF16))
                pT = e4.enter_context(nc.sbuf_tensor("pT", [128, 3, 16 * 512], BF16))
                linv = e4.enter_context(nc.sbuf_tensor("linv", [128, 2, 512], F32))
                ost = e4.enter_context(nc.sbuf_tensor("ost", [128, 2, 512], BF16))
                acc = e4.enter_context(nc.sbuf_tensor("acc", [128, 2, 1024], BF16))
                acc5 = e4.enter_context(nc.sbuf_tensor("acc5", [128, 2, 512], BF16))
                lntmp = e4.enter_context(nc.sbuf_tensor("lntmp", [128, 512], F32))
                ps_sc = e4.enter_context(nc.psum_tensor("ps_sc", [128, 4, 512], F32))
                ps_l = e4.enter_context(nc.psum_tensor("ps_l", [128, 2, 512], F32))
                ps_o = e4.enter_context(nc.psum_tensor("ps_o", [128, 2, 512], F32))
                NIT = NH * G
                t0, a0, v0, big0 = cnt["sT"], cnt["sA"], cnt["sV"], cnt["sBig"]
                p0 = cnt["sP"]
                sV2 = nc.alloc_semaphore("sV2")

                def exp_pos(i, m):
                    if i == 0:
                        return a0 + m + 1
                    return a0 + (10 * i - 1 + m if m < 2 else 10 * i + m + 1)

                def expinv_pos(k):
                    return a0 + (10 * (k + 1) + 2 if k <= NIT - 2 else 10 * NIT)
                qfill0 = list(fills["sQd"])
                ofill0 = list(fills["sOd"])

                _sid = nc.enter_named_scope("P4_attn", False)[0]
                with nc.Block() as b:
                    @b.sync
                    def _(s):
                        s.dma_start(K_res[:].rearrange("p (h s) -> p h s", h=NKV),
                                    Ks.rearrange("h p s -> p h s")).then_inc(sBig, 16)
                        s.dma_start(V_res[:].rearrange("p (j d) -> p j d", j=TS),
                                    Vs.rearrange("j p d -> p j d")).then_inc(sBig, 16)
                        for i in range(NIT):
                            if i >= 2:
                                ip = i - 2
                                s.wait_ge(sT, t0 + (16 if ip == 0 else 17 * ip + 15))
                            h, gg = i // G, i % G
                            s.dma_start(qb[:, i % 2, :],
                                        Qs[h, :, gg * 512:(gg + 1) * 512]
                                        ).then_inc(sQd[i % 2], 16)
                        for sl in range(2):
                            s.wait_ge(sQd[sl], 16 * (qfill0[sl] + NIT // 2))
                        s.wait_ge(sBig, big0 + 32)
                    cnt["sBig"] += 32
                    for i in range(NIT):
                        fills["sQd"][i % 2] += 1

                    def emit_score(t_, i, j):
                        kvh = (i // G) // 4
                        sidx = 16 * i + j
                        if sidx >= 4:
                            c = sidx - 4
                            t_.wait_ge(sA, exp_pos(c // 16, (c % 16) // 2))
                        t_.matmul(ps_sc[:, sidx % 4, :],
                                  K_res[:, kvh * S + j * 128:
                                        kvh * S + (j + 1) * 128],
                                  qb[:, i % 2, :],
                                  start=True, stop=True).then_inc(sT, 1)

                    def emit_lsum(t_, i):
                        t_.wait_ge(sV2, 8 * (i + 1))
                        if i >= 2:
                            t_.wait_ge(sV, v0 + 2 * (i - 1))
                        t_.matmul(ps_l[:, i % 2, :], ones[:], acc5[:, i % 2, :],
                                  start=True, stop=True)

                    def emit_pv(t_, i, j):
                        kvh = (i // G) // 4
                        ins = t_.matmul(
                            ps_o[:, i % 2, :],
                            V_res[:, j * DKV + kvh * 128:
                                  j * DKV + (kvh + 1) * 128],
                            pT[:, i % 3, j * 512:(j + 1) * 512],
                            start=(j == 0), stop=(j == 15))
                        if j == 15:
                            ins.then_inc(sT, 1)

                    @b.tensor
                    def _(t_):
                        t_.wait_ge(sBig, big0 + 32)
                        for i in range(NIT):
                            t_.wait_ge(sQd[i % 2], 16 * (qfill0[i % 2] + i // 2 + 1))
                            for j in range(6):
                                emit_score(t_, i, j)
                            if i >= 1:
                                emit_lsum(t_, i - 1)
                            for j in range(6, 16):
                                emit_score(t_, i, j)
                                if i >= 1:
                                    emit_pv(t_, i - 1, j - 6)
                            if i >= 1:
                                for j in range(10, 16):
                                    emit_pv(t_, i - 1, j)
                        emit_lsum(t_, NIT - 1)
                        for j in range(16):
                            emit_pv(t_, NIT - 1, j)
                    cnt["sT"] += 17 * NIT

                    @b.scalar
                    def _(sc):
                        def ln_expinv(k):
                            if k <= NIT - 2:
                                mark = 23 if k == 0 else 17 * (k + 1) + 6
                            else:
                                mark = 17 * NIT
                            sc.wait_ge(sT, t0 + mark)
                            sc.activation(lntmp[:], ps_l[:, k % 2, :],
                                          AF.Ln).then_inc(sA, 1)
                            sc.wait_ge(sA, expinv_pos(k) - 1)
                            sc.activation(linv[:, k % 2, :], lntmp[:], AF.Exp,
                                          scale=cst[:, 3:4]).then_inc(sA, 1)
                        for i in range(NIT):
                            for m in range(8):
                                if i >= 1 and m == 2:
                                    ln_expinv(i - 1)
                                j = 2 * m
                                sidx = 16 * i + j
                                if i >= 3 and m == 0:
                                    sc.wait_ge(sT, t0 + 17 * i - 18)
                                if i == 0:
                                    sc.wait_ge(sT, t0 + j + 2)
                                elif i == 1:
                                    sc.wait_ge(sT, t0 + 17 + j + 1)
                                else:
                                    sc.wait_ge(sT, t0 + 17 * i + j + 1)
                                sc.activation(pT[:, i % 3, j * 512:(j + 2) * 512],
                                              ps_sc[:, sidx % 4:sidx % 4 + 2, :],
                                              AF.Exp,
                                              bias=cst[:, 1:2]).then_inc(sA, 1)
                        ln_expinv(NIT - 1)
                    cnt["sA"] += 10 * NIT

                    def norm_tt(v, i):
                        lend = 17 * i + 33 if i <= NIT - 2 else 17 * NIT
                        v.wait_ge(sT, t0 + lend)
                        v.wait_ge(sA, expinv_pos(i)).then_inc(sV, 1)
                        if i >= 2:
                            v.wait_ge(sOd[i % 2],
                                      16 * (ofill0[i % 2] + (i - 2) // 2 + 1))
                        v.tensor_tensor(ost[:, i % 2, :], ps_o[:, i % 2, :],
                                        linv[:, i % 2, :],
                                        AluOpType.mult).then_inc(sV, 1)

                    @b.vector
                    def _(v):
                        for i in range(NIT):
                            if i >= 2:
                                v.wait_ge(sT, t0 + 17 * (i - 2) + 33)
                            for k in range(1, 8):
                                v.wait_ge(sA, exp_pos(i, k))
                                if k > 1:
                                    v.wait_ge(sV2, 8 * i + k - 1)
                                if k == 1:
                                    v.tensor_tensor(acc[:, i % 2, :],
                                                    pT[:, i % 3, 0:1024],
                                                    pT[:, i % 3, 1024:2048],
                                                    AluOpType.add).then_inc(sV2, 1)
                                else:
                                    v.tensor_tensor(acc[:, i % 2, :],
                                                    acc[:, i % 2, :],
                                                    pT[:, i % 3, k * 1024:(k + 1) * 1024],
                                                    AluOpType.add).then_inc(sV2, 1)
                            v.wait_ge(sV2, 8 * i + 7)
                            v.tensor_tensor(acc5[:, i % 2, :], acc[:, i % 2, 0:512],
                                            acc[:, i % 2, 512:1024],
                                            AluOpType.add).then_inc(sV2, 1)
                            if i >= 1:
                                norm_tt(v, i - 1)
                        norm_tt(v, NIT - 1)
                    cnt["sV"] += 2 * NIT

                    @b.gpsimd
                    def _(g):
                        for i in range(NIT):
                            h, gg = i // G, i % G
                            g.wait_ge(sV, v0 + 2 * (i + 1))
                            g.dma_start(Os[h, :, gg * 512:(gg + 1) * 512],
                                        ost[:, i % 2, :]).then_inc(sOd[i % 2], 16)
                        for sl in range(2):
                            g.wait_ge(sOd[sl], 16 * (ofill0[sl] + NIT // 2))
                    for i in range(NIT):
                        fills["sOd"][i % 2] += 1
                nc.leave_named_scope("P4_attn", _sid, False)

        if upto >= 6:
            with contextlib.ExitStack() as e5:
                o_res = e5.enter_context(nc.sbuf_tensor("o_res", [128, C * S], BF16))
                pstage = e5.enter_context(nc.sbuf_tensor("pstage", [128, 2, 256], F32))
                xsl = e5.enter_context(nc.sbuf_tensor("xsl", [128, 4, 256], F32))
                ps_p = e5.enter_context(nc.psum_tensor("ps_p", [128, 4, 512], F32))
                n_blocks = 16
                n_groups = n_blocks * TS
                t0, v5b, big0 = cnt["sT"], cnt["sV"], cnt["sBig"]
                wfill0 = list(fills["sWb"])
                sfill0 = list(fills["sSt"])
                xfill0 = list(fills["sXl"])

                _sid = nc.enter_named_scope("P5_wo", False)[0]
                with nc.Block() as b:
                    @b.sync
                    def _(s):
                        s.dma_start(o_res[:].rearrange("p (c s) -> p c s", c=C),
                                    Os.rearrange("c p s -> p c s")).then_inc(sBig, 16)
                        for idx in range(n_groups):
                            blk, ts = idx // TS, idx % TS
                            s.wait_ge(sV, v5b + idx + 1)
                            s.dma_start(Ps[ts * 128:(ts + 1) * 128,
                                           blk * 256:(blk + 1) * 256],
                                        pstage[:, idx % 2, :]
                                        ).then_inc(sSt[idx % 2], 16)
                        for sl in range(2):
                            s.wait_ge(sSt[sl], 16 * (sfill0[sl] + n_groups // 2))
                        s.wait_ge(sBig, big0 + 16)
                    cnt["sBig"] += 16
                    for idx in range(n_groups):
                        fills["sSt"][idx % 2] += 1

                    @b.scalar
                    def _(sc):
                        for idx in range(n_groups):
                            blk, ts = idx // TS, idx % TS
                            if idx >= 4:
                                sc.wait_ge(sV, v5b + idx - 3)
                            sc.dma_start(xsl[:, idx % 4, :],
                                         x[ts * 128:(ts + 1) * 128,
                                           blk * 256:(blk + 1) * 256]
                                         ).then_inc(sXl[idx % 4], 16)
                    for idx in range(n_groups):
                        fills["sXl"][idx % 4] += 1

                    @b.gpsimd
                    def _(g):
                        for blk in range(n_blocks):
                            if blk >= 2:
                                g.wait_ge(sT, t0 + TS * (blk - 1))
                            src = wo[:, blk * 256:(blk + 1) * 256].rearrange(
                                "(c p) m -> p c m", p=128)
                            g.dma_start(wbuf[:, blk % 2, :, :],
                                        src).then_inc(sWb[blk % 2], 16)
                        for sl in range(2):
                            g.wait_ge(sWb[sl], 16 * (wfill0[sl] + n_blocks // 2))
                    for blk in range(n_blocks):
                        fills["sWb"][blk % 2] += 1

                    @b.tensor
                    def _(t_):
                        t_.wait_ge(sBig, big0 + 16)
                        for blk in range(n_blocks):
                            t_.wait_ge(sWb[blk % 2],
                                       16 * (wfill0[blk % 2] + blk // 2 + 1))
                            for ts in range(TS):
                                idx = blk * TS + ts
                                if idx >= 4:
                                    t_.wait_ge(sV, v5b + idx - 3)
                                last = None
                                for c in range(C):
                                    last = t_.matmul(
                                        ps_p[:, idx % 4, 0:256],
                                        o_res[:, c * S + ts * 128:
                                              c * S + (ts + 1) * 128],
                                        wbuf[:, blk % 2, c, :],
                                        start=(c == 0), stop=(c == C - 1))
                                last.then_inc(sT, 1)
                    cnt["sT"] += n_groups

                    @b.vector
                    def _(v):
                        for idx in range(n_groups):
                            v.wait_ge(sT, t0 + idx + 1)
                            v.wait_ge(sXl[idx % 4],
                                      16 * (xfill0[idx % 4] + idx // 4 + 1))
                            if idx >= 2:
                                v.wait_ge(sSt[idx % 2],
                                          16 * (sfill0[idx % 2] + (idx - 2) // 2 + 1))
                            v.tensor_tensor(pstage[:, idx % 2, :],
                                            ps_p[:, idx % 4, 0:256],
                                            xsl[:, idx % 4, :],
                                            AluOpType.add).then_inc(sV, 1)
                    cnt["sV"] += n_groups
                nc.leave_named_scope("P5_wo", _sid, False)

        if upto >= 7:
            with contextlib.ExitStack() as e6:
                xn2 = e6.enter_context(nc.sbuf_tensor("xn2", [128, C * S], BF16))
                ATb = e6.enter_context(nc.sbuf_tensor("ATb", [128, C * 128], BF16))
                BSb = e6.enter_context(nc.sbuf_tensor("BSb", [128, D], BF16))
                bi_b = e6.enter_context(nc.sbuf_tensor("bi_b", [1, D], BF16))
                lw_sb = e6.enter_context(nc.sbuf_tensor("lw_sb", [128, 1], F32))
                h_all = e6.enter_context(nc.sbuf_tensor("h_all", [128, S], BF16))
                rsl = e6.enter_context(nc.sbuf_tensor("rsl", [128, 2, 256], F32))
                yst = e6.enter_context(nc.sbuf_tensor("yst", [128, 2, 256], F32))
                ps_f = e6.enter_context(nc.psum_tensor("ps_f", [128, 4, 512], F32))
                n_blocks = 16
                NID = n_blocks * TS
                t0, a0, v0, p0, lc0 = (cnt["sT"], cnt["sA"], cnt["sV"], cnt["sP"],
                                       cnt["sLC"])
                wfill0 = list(fills["sWb"])
                rfill0 = list(fills["sR"])
                yfill0 = list(fills["sYd"])

                _sid = nc.enter_named_scope("P6_ffn", False)[0]

                def wt_load(g, blk):
                    if blk >= 2:
                        g.wait_ge(sT, t0 + 4 + TS * (blk - 1))
                    src = WT[:, blk * 256:(blk + 1) * 256].rearrange(
                        "(c p) m -> p c m", p=128)
                    g.dma_start(wbuf[:, blk % 2, :, :],
                                src).then_inc(sWb[blk % 2], 16)

                with nc.Block() as b:
                    @b.gpsimd
                    def _(g):
                        g.dma_start(xn2[:], XNs[:]).then_inc(sLC, 16)
                        g.dma_start(ATb[:], AT.rearrange("(c p) m -> p c m", p=128)
                                    ).then_inc(sLC, 16)
                        g.dma_start(BSb[:], BS[:],
                                    max_dma_last_dim=8192).then_inc(sLC, 16)
                        g.dma_start(bi_b[:], bias[:],
                                    max_dma_last_dim=8192).then_inc(sLC, 16)
                        g.dma_start(lw_sb[:], lwr[:]).then_inc(sLC, 16)
                        wt_load(g, 0)
                        wt_load(g, 1)
                    cnt["sLC"] += 80
                    for blk in range(n_blocks):
                        fills["sWb"][blk % 2] += 1

                    @b.tensor
                    def _(t_):
                        t_.wait_ge(sLC, lc0 + 80)
                        for gg in range(4):
                            if gg >= 2:
                                t_.wait_ge(sV, v0 + gg - 1)
                            last = None
                            for c in range(C):
                                last = t_.matmul(
                                    ps_f[:, gg % 2, :],
                                    ATb[:, c * 128:(c + 1) * 128],
                                    xn2[:, c * S + gg * 512: c * S + (gg + 1) * 512],
                                    start=(c == 0), stop=(c == C - 1))
                            last.then_inc(sT, 1)
                        t_.wait_ge(sV, v0 + 4)
                        for blk in range(n_blocks):
                            t_.wait_ge(sWb[blk % 2],
                                       16 * (wfill0[blk % 2] + blk // 2 + 1))
                            for ts in range(TS):
                                idx = blk * TS + ts
                                if idx >= 4:
                                    t_.wait_ge(sV, v0 + idx + 1)
                                for c in range(C):
                                    t_.matmul(ps_f[:, idx % 4, 0:256],
                                              xn2[:, c * S + ts * 128:
                                                  c * S + (ts + 1) * 128],
                                              wbuf[:, blk % 2, c, :],
                                              start=(c == 0), stop=False)
                                t_.matmul(ps_f[:, idx % 4, 0:256],
                                          h_all[:, ts * 128:(ts + 1) * 128],
                                          BSb[:, blk * 256:(blk + 1) * 256],
                                          start=False, stop=False)
                                t_.matmul(ps_f[:, idx % 4, 0:256],
                                          ones[0:1, :],
                                          bi_b[:, blk * 256:(blk + 1) * 256],
                                          start=False, stop=True).then_inc(sT, 1)
                    cnt["sT"] += 4 + NID

                    @b.vector
                    def _(v):
                        for gg in range(4):
                            v.wait_ge(sT, t0 + gg + 1)
                            v.tensor_scalar(h_all[:, gg * 512:(gg + 1) * 512],
                                            ps_f[:, gg % 2, :], lw_sb[:], 2.0,
                                            AluOpType.mult,
                                            AluOpType.mult).then_inc(sV, 1)
                        rfill06 = list(fills["sR"])
                        for idx in range(NID):
                            v.wait_ge(sT, t0 + 4 + idx + 1)
                            v.wait_ge(sR[idx % 2],
                                      16 * (rfill06[idx % 2] + idx // 2 + 1))
                            if idx >= 2:
                                v.wait_ge(sYd[idx % 2],
                                          16 * (yfill0[idx % 2] + (idx - 2) // 2 + 1))
                            v.tensor_tensor(yst[:, idx % 2, :],
                                            ps_f[:, idx % 4, 0:256],
                                            rsl[:, idx % 2, :],
                                            AluOpType.add).then_inc(sV, 1)
                    cnt["sV"] += 4 + NID

                    @b.sync
                    def _(s):
                        for idx in range(NID):
                            blk, ts = idx // TS, idx % TS
                            if idx >= 2:
                                s.wait_ge(sV, v0 + 4 + idx - 1)
                            s.dma_start(rsl[:, idx % 2, :],
                                        Ps[ts * 128:(ts + 1) * 128,
                                           blk * 256:(blk + 1) * 256]
                                        ).then_inc(sR[idx % 2], 16)
                        for sl in range(2):
                            s.wait_ge(sR[sl], 16 * (rfill0[sl] + NID // 2))
                    for idx in range(NID):
                        fills["sR"][idx % 2] += 1

                    @b.gpsimd
                    def _(g):
                        for idx in range(NID):
                            if idx >= 11 and (idx - 11) % TS == 0:
                                blk = (idx - 11) // TS + 2
                                if blk < n_blocks:
                                    g.wait_ge(sV, v0 + 4 + idx)
                                    wt_load(g, blk)
                        for sl in range(2):
                            g.wait_ge(sWb[sl], 16 * (wfill0[sl] + n_blocks // 2))
                        g.wait_ge(sLC, lc0 + 80)

                    @b.scalar
                    def _(sc):
                        for idx in range(NID):
                            blk, ts = idx // TS, idx % TS
                            sc.wait_ge(sV, v0 + 4 + idx + 1)
                            sc.dma_start(y[ts * 128:(ts + 1) * 128,
                                           blk * 256:(blk + 1) * 256],
                                         yst[:, idx % 2, :]
                                         ).then_inc(sYd[idx % 2], 16)
                        for sl in range(2):
                            sc.wait_ge(sYd[sl], 16 * (yfill0[sl] + NID // 2))
                    for idx in range(NID):
                        fills["sYd"][idx % 2] += 1
                nc.leave_named_scope("P6_ffn", _sid, False)

    if dbg:
        ddone = nc.alloc_semaphore("ddone")
        pairs = [(dbg_outs["dXN"], XNs)]
        if upto >= 2:
            pairs.append((dbg_outs["dKs"], Ks))
        if upto >= 3:
            pairs.append((dbg_outs["dQs"], Qs))
        if upto >= 4:
            pairs.append((dbg_outs["dVs"], Vs))
        if upto >= 5:
            pairs.append((dbg_outs["dOs"], Os))
        if upto >= 6:
            pairs.append((dbg_outs["dPs"], Ps))
        with nc.Block() as b:
            @b.sync
            def _(s):
                for dst, srcap in pairs:
                    s.dma_start(dst[:], srcap[:]).then_inc(ddone, 16)
                s.wait_ge(ddone, 16 * len(pairs))

    return nc


def kernel(L0_attn_x, L0_attn_w_q, L0_attn_w_k, L0_attn_w_v, L0_attn_w_o,
           L0_ffn_lora_A, L0_ffn_lora_B, L0_ffn_W, L0_ffn_bias,
           L0_ffn_lora_weights):
    xs = np.ascontiguousarray(np.asarray(L0_attn_x, dtype=np.float32))
    wq_ = np.ascontiguousarray(np.asarray(L0_attn_w_q, dtype=np.float32))
    wk_ = np.ascontiguousarray(np.asarray(L0_attn_w_k, dtype=np.float32))
    wv_ = np.ascontiguousarray(np.asarray(L0_attn_w_v, dtype=np.float32))
    wo_ = np.ascontiguousarray(np.asarray(L0_attn_w_o, dtype=np.float32))
    WTm = np.ascontiguousarray(np.asarray(L0_ffn_W, dtype=np.float32).T)
    ATm = np.ascontiguousarray(
        np.asarray(L0_ffn_lora_A, dtype=np.float32).reshape(NR, D).T)
    BSm = np.ascontiguousarray(
        np.asarray(L0_ffn_lora_B, dtype=np.float32).transpose(0, 2, 1).reshape(NR, D))
    bias_ = np.ascontiguousarray(
        np.asarray(L0_ffn_bias, dtype=np.float32).reshape(1, D))
    lwr_ = np.ascontiguousarray(
        np.repeat(np.asarray(L0_ffn_lora_weights, dtype=np.float32), 16
                  ).reshape(NR, 1))
    ident = np.eye(128, dtype=np.float32)

    nc = build_nc()
    common = dict(wq=wq_, wk=wk_, wv=wv_, wo=wo_, WT=WTm, AT=ATm, BS=BSm,
                  bias=bias_, lwr=lwr_, ident=ident)
    in_maps = [dict(x=np.ascontiguousarray(xs[i]), **common) for i in range(8)]
    res = run_bass_kernel_spmd(nc, in_maps, core_ids=list(range(8)))
    out = np.stack([res.results[i]["y"] for i in range(8)], axis=0)
    return out


# version 12
# speedup vs baseline: 1.1024x; 1.0003x over previous
import contextlib
import numpy as np
import concourse.bass as bass
import concourse.mybir as mybir
from concourse.bass_utils import run_bass_kernel_spmd
from concourse.alu_op_type import AluOpType

F32 = mybir.dt.float32
BF16 = mybir.dt.bfloat16
AF = mybir.ActivationFunctionType

S = 2048
D = 4096
C = D // 128
HD = 128
NH = 32
NKV = 8
DKV = NKV * HD
G = S // 512
TS = S // 128
NR = 128
EXP_SHIFT = -8.0
QSCALE = 1.0 / np.sqrt(128.0)


def build_nc(upto=7, dbg=False):
    nc = bass.Bass("TRN2", target_bir_lowering=False)

    x = nc.dram_tensor("x", [S, D], F32, kind="ExternalInput").ap()
    wq = nc.dram_tensor("wq", [D, D], F32, kind="ExternalInput").ap()
    wk = nc.dram_tensor("wk", [D, DKV], F32, kind="ExternalInput").ap()
    wv = nc.dram_tensor("wv", [D, DKV], F32, kind="ExternalInput").ap()
    wo = nc.dram_tensor("wo", [D, D], F32, kind="ExternalInput").ap()
    WT = nc.dram_tensor("WT", [D, D], F32, kind="ExternalInput").ap()
    AT = nc.dram_tensor("AT", [D, NR], F32, kind="ExternalInput").ap()
    BS = nc.dram_tensor("BS", [NR, D], F32, kind="ExternalInput").ap()
    bias = nc.dram_tensor("bias", [1, D], F32, kind="ExternalInput").ap()
    lwr = nc.dram_tensor("lwr", [NR, 1], F32, kind="ExternalInput").ap()
    ident = nc.dram_tensor("ident", [128, 128], F32, kind="ExternalInput").ap()

    y = nc.dram_tensor("y", [S, D], F32, kind="ExternalOutput").ap()

    Qs = nc.dram_tensor("Qs", [NH, 128, S], BF16, kind="Internal").ap()
    Ks = nc.dram_tensor("Ks", [NKV, 128, S], BF16, kind="Internal").ap()
    Vs = nc.dram_tensor("Vs", [TS, 128, DKV], BF16, kind="Internal").ap()
    Os = nc.dram_tensor("Os", [C, 128, S], BF16, kind="Internal").ap()
    XNs = nc.dram_tensor("XNs", [128, C * S], BF16, kind="Internal").ap()
    Ps = nc.dram_tensor("Ps", [S, D], F32, kind="Internal").ap()

    dbg_outs = {}
    if dbg:
        for nm, shp, dt in [("dQs", [NH, 128, S], BF16), ("dKs", [NKV, 128, S], BF16),
                            ("dVs", [TS, 128, DKV], BF16), ("dXN", [128, C * S], BF16),
                            ("dOs", [C, 128, S], BF16), ("dPs", [S, D], F32)]:
            dbg_outs[nm] = nc.dram_tensor(nm, shp, dt, kind="ExternalOutput").ap()

    sLx = [nc.alloc_semaphore(f"sLx{i}") for i in range(2)]
    sWb = [nc.alloc_semaphore(f"sWb{i}") for i in range(2)]
    sLC = nc.alloc_semaphore("sLC")
    sBig = nc.alloc_semaphore("sBig")
    sSt = [nc.alloc_semaphore(f"sSt{i}") for i in range(2)]
    sQd = [nc.alloc_semaphore(f"sQd{i}") for i in range(2)]
    sOd = [nc.alloc_semaphore(f"sOd{i}") for i in range(2)]
    sR = [nc.alloc_semaphore(f"sR{i}") for i in range(2)]
    sYd = [nc.alloc_semaphore(f"sYd{i}") for i in range(2)]
    sXl = [nc.alloc_semaphore(f"sXl{i}") for i in range(4)]
    sV = nc.alloc_semaphore("sV")
    sA = nc.alloc_semaphore("sA")
    sT = nc.alloc_semaphore("sT")
    sP = nc.alloc_semaphore("sP")

    cnt = {"sV": 0, "sA": 0, "sT": 0, "sP": 0, "sLC": 0, "sBig": 0}
    fills = {"sLx": [0, 0], "sWb": [0, 0], "sSt": [0, 0], "sQd": [0, 0],
             "sOd": [0, 0], "sR": [0, 0], "sYd": [0, 0], "sXl": [0, 0, 0, 0]}

    es = contextlib.ExitStack()
    with es:
        wbuf = es.enter_context(nc.sbuf_tensor("wbuf", [128, 2, 32, 256], BF16))
        idb = es.enter_context(nc.sbuf_tensor("idb", [128, 128], BF16))
        ones = es.enter_context(nc.sbuf_tensor("ones", [128, 128], BF16))
        cst = es.enter_context(nc.sbuf_tensor("cst", [128, 4], F32))

        exn = es.enter_context(contextlib.ExitStack())
        xn = exn.enter_context(nc.sbuf_tensor("xn", [128, C * S], BF16))

        with contextlib.ExitStack() as e0:
            xbuf = e0.enter_context(nc.sbuf_tensor("xbuf", [128, 2, D], F32))
            xnt = e0.enter_context(nc.sbuf_tensor("xnt", [128, D], BF16))
            stats = e0.enter_context(nc.sbuf_tensor("stats", [128, 48], F32))
            mv = e0.enter_context(nc.sbuf_tensor("mv", [128, 2], F32))
            std_ = e0.enter_context(nc.sbuf_tensor("std_", [128, 1], F32))
            rstd = e0.enter_context(nc.sbuf_tensor("rstd", [128, 1], F32))
            bneg = e0.enter_context(nc.sbuf_tensor("bneg", [128, 1], F32))
            ps_t = e0.enter_context(nc.psum_tensor("ps_t", [128, 2, 2048], BF16))

            v0, a0, t0 = cnt["sV"], cnt["sA"], cnt["sT"]
            lc0 = cnt["sLC"]
            _sid = nc.enter_named_scope("P0_ln", False)[0]
            with nc.Block() as b:
                @b.gpsimd
                def _(g):
                    g.dma_start(idb[:], ident[:]).then_inc(sLC, 16)
                    g.wait_ge(sLC, lc0 + 16)
                cnt["sLC"] += 16

                @b.sync
                def _(s):
                    for t in range(TS):
                        if t >= 2:
                            s.wait_ge(sA, a0 + (2 if t == 2 else 4 * (t - 2)))
                        s.dma_start(xbuf[:, t % 2, :],
                                    x[t * 128:(t + 1) * 128, :]
                                    ).then_inc(sLx[t % 2], 16)
                    s.wait_ge(sLx[0], 16 * (TS // 2))
                    s.wait_ge(sLx[1], 16 * (TS // 2))
                for t in range(TS):
                    fills["sLx"][t % 2] += 1

                @b.vector
                def _(v):
                    v.memset(ones[:], 1.0)
                    v.memset(cst[:, 0:1], 1e-5)
                    v.memset(cst[:, 1:2], EXP_SHIFT)
                    v.memset(cst[:, 2:3], QSCALE)
                    v.memset(cst[:, 3:4], -1.0).then_inc(sV, 1)
                    for t in range(TS):
                        v.wait_ge(sLx[t % 2], 16 * (t // 2 + 1))
                        last = None
                        for i in range(8):
                            last = v.bn_stats(stats[:, i * 6:(i + 1) * 6],
                                              xbuf[:, t % 2, i * 512:(i + 1) * 512])
                        last.then_inc(sV, 1)
                        v.wait_ge(sV, v0 + 1 + 4 * t + 1)
                        v.bn_aggr(mv[:], stats[:]).then_inc(sV, 1)
                        v.wait_ge(sA, a0 + (1 if t == 0 else 4 * t - 1))
                        if t >= 1:
                            v.wait_ge(sA, a0 + (2 if t == 1 else 4 * (t - 1)))
                        v.reciprocal(rstd[:], std_[:]).then_inc(sV, 1)
                        v.wait_ge(sV, v0 + 1 + 4 * t + 3)
                        v.tensor_scalar(bneg[:], mv[:, 0:1], rstd[:], -1.0,
                                        AluOpType.mult,
                                        AluOpType.mult).then_inc(sV, 1)
                cnt["sV"] += 1 + 4 * TS

                @b.scalar
                def _(sc):
                    for t in range(TS):
                        sc.wait_ge(sV, v0 + 1 + 4 * t + 2)
                        sc.activation(std_[:], mv[:, 1:2], AF.Sqrt,
                                      bias=cst[:, 0:1]).then_inc(sA, 1)
                        sc.wait_ge(sV, v0 + 1 + 4 * t + 4)
                        sc.wait_ge(sLx[t % 2], 16 * (t // 2 + 1))
                        if t >= 1:
                            sc.wait_ge(sT, t0 + 2 * t)
                        sc.activation(xnt[:], xbuf[:, t % 2, :], AF.Identity,
                                      bias=bneg[:], scale=rstd[:]).then_inc(sA, 1)
                        if t >= 1:
                            for half in range(2):
                                sc.wait_ge(sT, t0 + 2 * (t - 1) + half + 1)
                                dst = xn[:].rearrange("p (c s) -> p c s", c=C)[
                                    :, half * 16:(half + 1) * 16,
                                    (t - 1) * 128: t * 128]
                                src = ps_t[:, half, :].rearrange(
                                    "p (c s) -> p c s", c=16)
                                sc.activation(dst, src, AF.Copy, bias=0.0,
                                              scale=1.0).then_inc(sA, 1)
                    for half in range(2):
                        sc.wait_ge(sT, t0 + 2 * (TS - 1) + half + 1)
                        dst = xn[:].rearrange("p (c s) -> p c s", c=C)[
                            :, half * 16:(half + 1) * 16, (TS - 1) * 128: TS * 128]
                        src = ps_t[:, half, :].rearrange("p (c s) -> p c s", c=16)
                        sc.activation(dst, src, AF.Copy, bias=0.0,
                                      scale=1.0).then_inc(sA, 1)
                cnt["sA"] += 4 * TS

                @b.tensor
                def _(t_):
                    t_.wait_ge(sLC, lc0 + 16)
                    for t in range(TS):
                        t_.wait_ge(sA, a0 + (2 if t == 0 else 4 * t))
                        for half in range(2):
                            if t >= 1:
                                t_.wait_ge(sA, a0 + 4 * t + 1 + half)
                            last = None
                            for cc in range(16):
                                c = half * 16 + cc
                                last = t_.transpose(
                                    ps_t[:, half, cc * 128:(cc + 1) * 128],
                                    xnt[:, c * 128:(c + 1) * 128], idb[:])
                            last.then_inc(sT, 1)
                cnt["sT"] += 2 * TS
            nc.leave_named_scope("P0_ln", _sid, False)

        def proj_heads_phase(w_ap, n_blocks, out_s, tag):
            with contextlib.ExitStack() as e1:
                stage = e1.enter_context(
                    nc.sbuf_tensor(f"stg{tag}", [128, 2, 512], BF16))
                ps_k = e1.enter_context(
                    nc.psum_tensor(f"psk{tag}", [128, 2, 512], F32))
                n_groups = n_blocks * 8
                t0, a0 = cnt["sT"], cnt["sA"]
                wfill0 = list(fills["sWb"])
                sfill0 = list(fills["sSt"])

                _sid = nc.enter_named_scope("P12_" + tag, False)[0]
                with nc.Block() as b:
                    @b.gpsimd
                    def _(g):
                        for blk in range(n_blocks):
                            if blk >= 2:
                                g.wait_ge(sT, t0 + 8 * (blk - 1))
                            src = w_ap[:, blk * 256:(blk + 1) * 256].rearrange(
                                "(c p) m -> p c m", p=128)
                            g.dma_start(wbuf[:, blk % 2, :, :],
                                        src).then_inc(sWb[blk % 2], 16)
                        for sl in range(2):
                            nfill = (n_blocks - sl + 1) // 2
                            g.wait_ge(sWb[sl], 16 * (wfill0[sl] + nfill))
                    for blk in range(n_blocks):
                        fills["sWb"][blk % 2] += 1

                    @b.tensor
                    def _(t_):
                        for blk in range(n_blocks):
                            t_.wait_ge(sWb[blk % 2],
                                       16 * (wfill0[blk % 2] + blk // 2 + 1))
                            for hh in range(2):
                                for gg in range(4):
                                    idx = blk * 8 + hh * 4 + gg
                                    if idx >= 2:
                                        t_.wait_ge(sA, a0 + idx - 1)
                                    last = None
                                    for c in range(C):
                                        last = t_.matmul(
                                            ps_k[:, idx % 2, :],
                                            wbuf[:, blk % 2, c,
                                                 hh * 128:(hh + 1) * 128],
                                            xn[:, c * S + gg * 512:
                                               c * S + (gg + 1) * 512],
                                            start=(c == 0), stop=(c == C - 1))
                                    last.then_inc(sT, 1)
                    cnt["sT"] += n_groups

                    @b.scalar
                    def _(sc):
                        for idx in range(n_groups):
                            sc.wait_ge(sT, t0 + idx + 1)
                            if idx >= 2:
                                sc.wait_ge(sSt[idx % 2],
                                           16 * (sfill0[idx % 2] + (idx - 2) // 2 + 1))
                            sc.activation(stage[:, idx % 2, :],
                                          ps_k[:, idx % 2, :], AF.Copy, bias=0.0,
                                          scale=cst[:, 2:3] if tag == "q" else 1.0
                                          ).then_inc(sA, 1)
                    cnt["sA"] += n_groups

                    @b.sync
                    def _(s):
                        for idx in range(n_groups):
                            blk, hh, gg = idx // 8, (idx // 4) % 2, idx % 4
                            head = blk * 2 + hh
                            s.wait_ge(sA, a0 + idx + 1)
                            s.dma_start(out_s[head, :, gg * 512:(gg + 1) * 512],
                                        stage[:, idx % 2, :]
                                        ).then_inc(sSt[idx % 2], 16)
                        for sl in range(2):
                            nfill = (n_groups - sl + 1) // 2
                            s.wait_ge(sSt[sl], 16 * (sfill0[sl] + nfill))
                    for idx in range(n_groups):
                        fills["sSt"][idx % 2] += 1
                nc.leave_named_scope("P12_" + tag, _sid, False)

        if upto >= 2:
            proj_heads_phase(wk, NKV // 2, Ks, "k")
        if upto >= 3:
            proj_heads_phase(wq, NH // 2, Qs, "q")

        if upto >= 4:
            with contextlib.ExitStack() as e3:
                stage = e3.enter_context(nc.sbuf_tensor("stgv", [128, 2, 256], BF16))
                ps_v = e3.enter_context(nc.psum_tensor("psv", [128, 4, 512], F32))
                n_blocks = 4
                n_groups = n_blocks * TS
                t0, a0, big0 = cnt["sT"], cnt["sA"], cnt["sBig"]
                wfill0 = list(fills["sWb"])
                sfill0 = list(fills["sSt"])

                _sid = nc.enter_named_scope("P3_v", False)[0]
                with nc.Block() as b:
                    @b.gpsimd
                    def _(g):
                        for blk in range(n_blocks):
                            if blk >= 2:
                                g.wait_ge(sT, t0 + TS * (blk - 1))
                            src = wv[:, blk * 256:(blk + 1) * 256].rearrange(
                                "(c p) m -> p c m", p=128)
                            g.dma_start(wbuf[:, blk % 2, :, :],
                                        src).then_inc(sWb[blk % 2], 16)
                        for sl in range(2):
                            g.wait_ge(sWb[sl], 16 * (wfill0[sl] + 2))
                    for blk in range(n_blocks):
                        fills["sWb"][blk % 2] += 1

                    @b.tensor
                    def _(t_):
                        for blk in range(n_blocks):
                            t_.wait_ge(sWb[blk % 2],
                                       16 * (wfill0[blk % 2] + blk // 2 + 1))
                            for ts in range(TS):
                                idx = blk * TS + ts
                                if idx >= 4:
                                    t_.wait_ge(sA, a0 + idx - 3)
                                last = None
                                for c in range(C):
                                    last = t_.matmul(
                                        ps_v[:, idx % 4, 0:256],
                                        xn[:, c * S + ts * 128:
                                           c * S + (ts + 1) * 128],
                                        wbuf[:, blk % 2, c, :],
                                        start=(c == 0), stop=(c == C - 1))
                                last.then_inc(sT, 1)
                    cnt["sT"] += n_groups

                    @b.scalar
                    def _(sc):
                        for idx in range(n_groups):
                            sc.wait_ge(sT, t0 + idx + 1)
                            if idx >= 2:
                                sc.wait_ge(sSt[idx % 2],
                                           16 * (sfill0[idx % 2] + (idx - 2) // 2 + 1))
                            sc.activation(stage[:, idx % 2, :],
                                          ps_v[:, idx % 4, 0:256],
                                          AF.Copy, bias=0.0,
                                          scale=1.0).then_inc(sA, 1)
                    cnt["sA"] += n_groups

                    @b.sync
                    def _(s):
                        s.dma_start(XNs[:], xn[:]).then_inc(sBig, 16)
                        for idx in range(n_groups):
                            blk, ts = idx // TS, idx % TS
                            s.wait_ge(sA, a0 + idx + 1)
                            s.dma_start(Vs[ts, :, blk * 256:(blk + 1) * 256],
                                        stage[:, idx % 2, :]
                                        ).then_inc(sSt[idx % 2], 16)
                        for sl in range(2):
                            s.wait_ge(sSt[sl], 16 * (sfill0[sl] + n_groups // 2))
                        s.wait_ge(sBig, big0 + 16)
                    cnt["sBig"] += 16
                    for idx in range(n_groups):
                        fills["sSt"][idx % 2] += 1
                nc.leave_named_scope("P3_v", _sid, False)
        elif dbg:
            dsp = nc.alloc_semaphore("dsp")
            with nc.Block() as b:
                @b.sync
                def _(s):
                    s.dma_start(XNs[:], xn[:]).then_inc(dsp, 16)
                    s.wait_ge(dsp, 16)

        exn.close()

        if upto >= 5:
            with contextlib.ExitStack() as e4:
                K_res = e4.enter_context(nc.sbuf_tensor("K_res", [128, NKV * S], BF16))
                V_res = e4.enter_context(nc.sbuf_tensor("V_res", [128, TS * DKV], BF16))
                qb = e4.enter_context(nc.sbuf_tensor("qb", [128, 2, 512], BF16))
                pT = e4.enter_context(nc.sbuf_tensor("pT", [128, 3, 16 * 512], BF16))
                linv = e4.enter_context(nc.sbuf_tensor("linv", [128, 2, 512], F32))
                ost = e4.enter_context(nc.sbuf_tensor("ost", [128, 2, 512], BF16))
                acc = e4.enter_context(nc.sbuf_tensor("acc", [128, 2, 1024], BF16))
                acc5 = e4.enter_context(nc.sbuf_tensor("acc5", [128, 2, 512], BF16))
                lntmp = e4.enter_context(nc.sbuf_tensor("lntmp", [128, 512], F32))
                ps_sc = e4.enter_context(nc.psum_tensor("ps_sc", [128, 4, 512], F32))
                ps_l = e4.enter_context(nc.psum_tensor("ps_l", [128, 2, 512], F32))
                ps_o = e4.enter_context(nc.psum_tensor("ps_o", [128, 2, 512], F32))
                NIT = NH * G
                t0, a0, v0, big0 = cnt["sT"], cnt["sA"], cnt["sV"], cnt["sBig"]
                p0 = cnt["sP"]
                sV2 = nc.alloc_semaphore("sV2")

                def exp_pos(i, m):
                    if i == 0:
                        return a0 + m + 1
                    return a0 + (10 * i - 1 + m if m < 2 else 10 * i + m + 1)

                def expinv_pos(k):
                    return a0 + (10 * (k + 1) + 2 if k <= NIT - 2 else 10 * NIT)
                qfill0 = list(fills["sQd"])
                ofill0 = list(fills["sOd"])

                _sid = nc.enter_named_scope("P4_attn", False)[0]
                with nc.Block() as b:
                    @b.sync
                    def _(s):
                        s.dma_start(K_res[:].rearrange("p (h s) -> p h s", h=NKV),
                                    Ks.rearrange("h p s -> p h s")).then_inc(sBig, 16)
                        s.dma_start(V_res[:].rearrange("p (j d) -> p j d", j=TS),
                                    Vs.rearrange("j p d -> p j d")).then_inc(sBig, 16)
                        for i in range(NIT):
                            if i >= 2:
                                ip = i - 2
                                s.wait_ge(sT, t0 + (16 if ip == 0 else 17 * ip + 15))
                            h, gg = i // G, i % G
                            s.dma_start(qb[:, i % 2, :],
                                        Qs[h, :, gg * 512:(gg + 1) * 512]
                                        ).then_inc(sQd[i % 2], 16)
                        for sl in range(2):
                            s.wait_ge(sQd[sl], 16 * (qfill0[sl] + NIT // 2))
                        s.wait_ge(sBig, big0 + 32)
                    cnt["sBig"] += 32
                    for i in range(NIT):
                        fills["sQd"][i % 2] += 1

                    def emit_score(t_, i, j):
                        kvh = (i // G) // 4
                        sidx = 16 * i + j
                        if sidx >= 4:
                            c = sidx - 4
                            t_.wait_ge(sA, exp_pos(c // 16, (c % 16) // 2))
                        t_.matmul(ps_sc[:, sidx % 4, :],
                                  K_res[:, kvh * S + j * 128:
                                        kvh * S + (j + 1) * 128],
                                  qb[:, i % 2, :],
                                  start=True, stop=True).then_inc(sT, 1)

                    def emit_lsum(t_, i):
                        t_.wait_ge(sV2, 8 * (i + 1))
                        if i >= 2:
                            t_.wait_ge(sV, v0 + 2 * (i - 1))
                        t_.matmul(ps_l[:, i % 2, :], ones[:], acc5[:, i % 2, :],
                                  start=True, stop=True)

                    def emit_pv(t_, i, j):
                        kvh = (i // G) // 4
                        ins = t_.matmul(
                            ps_o[:, i % 2, :],
                            V_res[:, j * DKV + kvh * 128:
                                  j * DKV + (kvh + 1) * 128],
                            pT[:, i % 3, j * 512:(j + 1) * 512],
                            start=(j == 0), stop=(j == 15))
                        if j == 15:
                            ins.then_inc(sT, 1)

                    @b.tensor
                    def _(t_):
                        t_.wait_ge(sBig, big0 + 32)
                        for i in range(NIT):
                            t_.wait_ge(sQd[i % 2], 16 * (qfill0[i % 2] + i // 2 + 1))
                            for j in range(6):
                                emit_score(t_, i, j)
                            if i >= 1:
                                emit_lsum(t_, i - 1)
                            for j in range(6, 16):
                                emit_score(t_, i, j)
                                if i >= 1:
                                    emit_pv(t_, i - 1, j - 6)
                            if i >= 1:
                                for j in range(10, 16):
                                    emit_pv(t_, i - 1, j)
                        emit_lsum(t_, NIT - 1)
                        for j in range(16):
                            emit_pv(t_, NIT - 1, j)
                    cnt["sT"] += 17 * NIT

                    @b.scalar
                    def _(sc):
                        def ln_expinv(k):
                            if k <= NIT - 2:
                                mark = 23 if k == 0 else 17 * (k + 1) + 6
                            else:
                                mark = 17 * NIT
                            sc.wait_ge(sT, t0 + mark)
                            sc.activation(lntmp[:], ps_l[:, k % 2, :],
                                          AF.Ln).then_inc(sA, 1)
                            sc.wait_ge(sA, expinv_pos(k) - 1)
                            sc.activation(linv[:, k % 2, :], lntmp[:], AF.Exp,
                                          scale=cst[:, 3:4]).then_inc(sA, 1)
                        for i in range(NIT):
                            for m in range(8):
                                if i >= 1 and m == 2:
                                    ln_expinv(i - 1)
                                j = 2 * m
                                sidx = 16 * i + j
                                if i >= 3 and m == 0:
                                    sc.wait_ge(sT, t0 + 17 * i - 18)
                                if i == 0:
                                    sc.wait_ge(sT, t0 + j + 2)
                                elif i == 1:
                                    sc.wait_ge(sT, t0 + 17 + j + 1)
                                else:
                                    sc.wait_ge(sT, t0 + 17 * i + j + 1)
                                sc.activation(pT[:, i % 3, j * 512:(j + 2) * 512],
                                              ps_sc[:, sidx % 4:sidx % 4 + 2, :],
                                              AF.Exp,
                                              bias=cst[:, 1:2]).then_inc(sA, 1)
                        ln_expinv(NIT - 1)
                    cnt["sA"] += 10 * NIT

                    def norm_tt(v, i):
                        lend = 17 * i + 33 if i <= NIT - 2 else 17 * NIT
                        v.wait_ge(sT, t0 + lend)
                        v.wait_ge(sA, expinv_pos(i))
                        if i >= 2:
                            v.wait_ge(sOd[i % 2],
                                      16 * (ofill0[i % 2] + (i - 2) // 2 + 1))
                        v.tensor_tensor(ost[:, i % 2, :], ps_o[:, i % 2, :],
                                        linv[:, i % 2, :],
                                        AluOpType.mult).then_inc(sV, 2)

                    @b.vector
                    def _(v):
                        for i in range(NIT):
                            if i >= 2:
                                v.wait_ge(sT, t0 + 17 * (i - 2) + 33)
                            for k in range(1, 8):
                                v.wait_ge(sA, exp_pos(i, k))
                                if k > 1:
                                    v.wait_ge(sV2, 8 * i + k - 1)
                                if k == 1:
                                    v.tensor_tensor(acc[:, i % 2, :],
                                                    pT[:, i % 3, 0:1024],
                                                    pT[:, i % 3, 1024:2048],
                                                    AluOpType.add).then_inc(sV2, 1)
                                else:
                                    v.tensor_tensor(acc[:, i % 2, :],
                                                    acc[:, i % 2, :],
                                                    pT[:, i % 3, k * 1024:(k + 1) * 1024],
                                                    AluOpType.add).then_inc(sV2, 1)
                            v.wait_ge(sV2, 8 * i + 7)
                            v.tensor_tensor(acc5[:, i % 2, :], acc[:, i % 2, 0:512],
                                            acc[:, i % 2, 512:1024],
                                            AluOpType.add).then_inc(sV2, 1)
                            if i >= 1:
                                norm_tt(v, i - 1)
                        norm_tt(v, NIT - 1)
                    cnt["sV"] += 2 * NIT

                    @b.gpsimd
                    def _(g):
                        for i in range(NIT):
                            h, gg = i // G, i % G
                            g.wait_ge(sV, v0 + 2 * (i + 1))
                            g.dma_start(Os[h, :, gg * 512:(gg + 1) * 512],
                                        ost[:, i % 2, :]).then_inc(sOd[i % 2], 16)
                        for sl in range(2):
                            g.wait_ge(sOd[sl], 16 * (ofill0[sl] + NIT // 2))
                    for i in range(NIT):
                        fills["sOd"][i % 2] += 1
                nc.leave_named_scope("P4_attn", _sid, False)

        if upto >= 6:
            with contextlib.ExitStack() as e5:
                o_res = e5.enter_context(nc.sbuf_tensor("o_res", [128, C * S], BF16))
                pstage = e5.enter_context(nc.sbuf_tensor("pstage", [128, 2, 256], F32))
                xsl = e5.enter_context(nc.sbuf_tensor("xsl", [128, 4, 256], F32))
                ps_p = e5.enter_context(nc.psum_tensor("ps_p", [128, 4, 512], F32))
                n_blocks = 16
                n_groups = n_blocks * TS
                t0, v5b, big0 = cnt["sT"], cnt["sV"], cnt["sBig"]
                wfill0 = list(fills["sWb"])
                sfill0 = list(fills["sSt"])
                xfill0 = list(fills["sXl"])

                _sid = nc.enter_named_scope("P5_wo", False)[0]
                with nc.Block() as b:
                    @b.sync
                    def _(s):
                        s.dma_start(o_res[:].rearrange("p (c s) -> p c s", c=C),
                                    Os.rearrange("c p s -> p c s")).then_inc(sBig, 16)
                        for idx in range(n_groups):
                            blk, ts = idx // TS, idx % TS
                            s.wait_ge(sV, v5b + idx + 1)
                            s.dma_start(Ps[ts * 128:(ts + 1) * 128,
                                           blk * 256:(blk + 1) * 256],
                                        pstage[:, idx % 2, :]
                                        ).then_inc(sSt[idx % 2], 16)
                        for sl in range(2):
                            s.wait_ge(sSt[sl], 16 * (sfill0[sl] + n_groups // 2))
                        s.wait_ge(sBig, big0 + 16)
                    cnt["sBig"] += 16
                    for idx in range(n_groups):
                        fills["sSt"][idx % 2] += 1

                    @b.scalar
                    def _(sc):
                        for idx in range(n_groups):
                            blk, ts = idx // TS, idx % TS
                            if idx >= 4:
                                sc.wait_ge(sV, v5b + idx - 3)
                            sc.dma_start(xsl[:, idx % 4, :],
                                         x[ts * 128:(ts + 1) * 128,
                                           blk * 256:(blk + 1) * 256]
                                         ).then_inc(sXl[idx % 4], 16)
                    for idx in range(n_groups):
                        fills["sXl"][idx % 4] += 1

                    @b.gpsimd
                    def _(g):
                        for blk in range(n_blocks):
                            if blk >= 2:
                                g.wait_ge(sT, t0 + TS * (blk - 1))
                            src = wo[:, blk * 256:(blk + 1) * 256].rearrange(
                                "(c p) m -> p c m", p=128)
                            g.dma_start(wbuf[:, blk % 2, :, :],
                                        src).then_inc(sWb[blk % 2], 16)
                        for sl in range(2):
                            g.wait_ge(sWb[sl], 16 * (wfill0[sl] + n_blocks // 2))
                    for blk in range(n_blocks):
                        fills["sWb"][blk % 2] += 1

                    @b.tensor
                    def _(t_):
                        t_.wait_ge(sBig, big0 + 16)
                        for blk in range(n_blocks):
                            t_.wait_ge(sWb[blk % 2],
                                       16 * (wfill0[blk % 2] + blk // 2 + 1))
                            for ts in range(TS):
                                idx = blk * TS + ts
                                if idx >= 4:
                                    t_.wait_ge(sV, v5b + idx - 3)
                                last = None
                                for c in range(C):
                                    last = t_.matmul(
                                        ps_p[:, idx % 4, 0:256],
                                        o_res[:, c * S + ts * 128:
                                              c * S + (ts + 1) * 128],
                                        wbuf[:, blk % 2, c, :],
                                        start=(c == 0), stop=(c == C - 1))
                                last.then_inc(sT, 1)
                    cnt["sT"] += n_groups

                    @b.vector
                    def _(v):
                        for idx in range(n_groups):
                            v.wait_ge(sT, t0 + idx + 1)
                            v.wait_ge(sXl[idx % 4],
                                      16 * (xfill0[idx % 4] + idx // 4 + 1))
                            if idx >= 2:
                                v.wait_ge(sSt[idx % 2],
                                          16 * (sfill0[idx % 2] + (idx - 2) // 2 + 1))
                            v.tensor_tensor(pstage[:, idx % 2, :],
                                            ps_p[:, idx % 4, 0:256],
                                            xsl[:, idx % 4, :],
                                            AluOpType.add).then_inc(sV, 1)
                    cnt["sV"] += n_groups
                nc.leave_named_scope("P5_wo", _sid, False)

        if upto >= 7:
            with contextlib.ExitStack() as e6:
                xn2 = e6.enter_context(nc.sbuf_tensor("xn2", [128, C * S], BF16))
                ATb = e6.enter_context(nc.sbuf_tensor("ATb", [128, C * 128], BF16))
                BSb = e6.enter_context(nc.sbuf_tensor("BSb", [128, D], BF16))
                bi_b = e6.enter_context(nc.sbuf_tensor("bi_b", [1, D], BF16))
                lw_sb = e6.enter_context(nc.sbuf_tensor("lw_sb", [128, 1], F32))
                h_all = e6.enter_context(nc.sbuf_tensor("h_all", [128, S], BF16))
                rsl = e6.enter_context(nc.sbuf_tensor("rsl", [128, 2, 256], F32))
                yst = e6.enter_context(nc.sbuf_tensor("yst", [128, 2, 256], F32))
                ps_f = e6.enter_context(nc.psum_tensor("ps_f", [128, 4, 512], F32))
                n_blocks = 16
                NID = n_blocks * TS
                t0, a0, v0, p0, lc0 = (cnt["sT"], cnt["sA"], cnt["sV"], cnt["sP"],
                                       cnt["sLC"])
                wfill0 = list(fills["sWb"])
                rfill0 = list(fills["sR"])
                yfill0 = list(fills["sYd"])

                _sid = nc.enter_named_scope("P6_ffn", False)[0]

                def wt_load(g, blk):
                    if blk >= 2:
                        g.wait_ge(sT, t0 + 4 + TS * (blk - 1))
                    src = WT[:, blk * 256:(blk + 1) * 256].rearrange(
                        "(c p) m -> p c m", p=128)
                    g.dma_start(wbuf[:, blk % 2, :, :],
                                src).then_inc(sWb[blk % 2], 16)

                with nc.Block() as b:
                    @b.gpsimd
                    def _(g):
                        g.dma_start(xn2[:], XNs[:]).then_inc(sLC, 16)
                        g.dma_start(ATb[:], AT.rearrange("(c p) m -> p c m", p=128)
                                    ).then_inc(sLC, 16)
                        g.dma_start(BSb[:], BS[:],
                                    max_dma_last_dim=8192).then_inc(sLC, 16)
                        g.dma_start(bi_b[:], bias[:],
                                    max_dma_last_dim=8192).then_inc(sLC, 16)
                        g.dma_start(lw_sb[:], lwr[:]).then_inc(sLC, 16)
                        wt_load(g, 0)
                        wt_load(g, 1)
                    cnt["sLC"] += 80
                    for blk in range(n_blocks):
                        fills["sWb"][blk % 2] += 1

                    @b.tensor
                    def _(t_):
                        t_.wait_ge(sLC, lc0 + 80)
                        for gg in range(4):
                            if gg >= 2:
                                t_.wait_ge(sV, v0 + gg - 1)
                            last = None
                            for c in range(C):
                                last = t_.matmul(
                                    ps_f[:, gg % 2, :],
                                    ATb[:, c * 128:(c + 1) * 128],
                                    xn2[:, c * S + gg * 512: c * S + (gg + 1) * 512],
                                    start=(c == 0), stop=(c == C - 1))
                            last.then_inc(sT, 1)
                        t_.wait_ge(sV, v0 + 4)
                        for blk in range(n_blocks):
                            t_.wait_ge(sWb[blk % 2],
                                       16 * (wfill0[blk % 2] + blk // 2 + 1))
                            for ts in range(TS):
                                idx = blk * TS + ts
                                if idx >= 4:
                                    t_.wait_ge(sV, v0 + idx + 1)
                                for c in range(C):
                                    t_.matmul(ps_f[:, idx % 4, 0:256],
                                              xn2[:, c * S + ts * 128:
                                                  c * S + (ts + 1) * 128],
                                              wbuf[:, blk % 2, c, :],
                                              start=(c == 0), stop=False)
                                t_.matmul(ps_f[:, idx % 4, 0:256],
                                          h_all[:, ts * 128:(ts + 1) * 128],
                                          BSb[:, blk * 256:(blk + 1) * 256],
                                          start=False, stop=False)
                                t_.matmul(ps_f[:, idx % 4, 0:256],
                                          ones[0:1, :],
                                          bi_b[:, blk * 256:(blk + 1) * 256],
                                          start=False, stop=True).then_inc(sT, 1)
                    cnt["sT"] += 4 + NID

                    @b.vector
                    def _(v):
                        for gg in range(4):
                            v.wait_ge(sT, t0 + gg + 1)
                            v.tensor_scalar(h_all[:, gg * 512:(gg + 1) * 512],
                                            ps_f[:, gg % 2, :], lw_sb[:], 2.0,
                                            AluOpType.mult,
                                            AluOpType.mult).then_inc(sV, 1)
                        rfill06 = list(fills["sR"])
                        for idx in range(NID):
                            v.wait_ge(sT, t0 + 4 + idx + 1)
                            v.wait_ge(sR[idx % 2],
                                      16 * (rfill06[idx % 2] + idx // 2 + 1))
                            if idx >= 2:
                                v.wait_ge(sYd[idx % 2],
                                          16 * (yfill0[idx % 2] + (idx - 2) // 2 + 1))
                            v.tensor_tensor(yst[:, idx % 2, :],
                                            ps_f[:, idx % 4, 0:256],
                                            rsl[:, idx % 2, :],
                                            AluOpType.add).then_inc(sV, 1)
                    cnt["sV"] += 4 + NID

                    @b.sync
                    def _(s):
                        for idx in range(NID):
                            blk, ts = idx // TS, idx % TS
                            if idx >= 2:
                                s.wait_ge(sV, v0 + 4 + idx - 1)
                            s.dma_start(rsl[:, idx % 2, :],
                                        Ps[ts * 128:(ts + 1) * 128,
                                           blk * 256:(blk + 1) * 256]
                                        ).then_inc(sR[idx % 2], 16)
                        for sl in range(2):
                            s.wait_ge(sR[sl], 16 * (rfill0[sl] + NID // 2))
                    for idx in range(NID):
                        fills["sR"][idx % 2] += 1

                    @b.gpsimd
                    def _(g):
                        for idx in range(NID):
                            if idx >= 11 and (idx - 11) % TS == 0:
                                blk = (idx - 11) // TS + 2
                                if blk < n_blocks:
                                    g.wait_ge(sV, v0 + 4 + idx)
                                    wt_load(g, blk)
                        for sl in range(2):
                            g.wait_ge(sWb[sl], 16 * (wfill0[sl] + n_blocks // 2))
                        g.wait_ge(sLC, lc0 + 80)

                    @b.scalar
                    def _(sc):
                        for idx in range(NID):
                            blk, ts = idx // TS, idx % TS
                            sc.wait_ge(sV, v0 + 4 + idx + 1)
                            sc.dma_start(y[ts * 128:(ts + 1) * 128,
                                           blk * 256:(blk + 1) * 256],
                                         yst[:, idx % 2, :]
                                         ).then_inc(sYd[idx % 2], 16)
                        for sl in range(2):
                            sc.wait_ge(sYd[sl], 16 * (yfill0[sl] + NID // 2))
                    for idx in range(NID):
                        fills["sYd"][idx % 2] += 1
                nc.leave_named_scope("P6_ffn", _sid, False)

    if dbg:
        ddone = nc.alloc_semaphore("ddone")
        pairs = [(dbg_outs["dXN"], XNs)]
        if upto >= 2:
            pairs.append((dbg_outs["dKs"], Ks))
        if upto >= 3:
            pairs.append((dbg_outs["dQs"], Qs))
        if upto >= 4:
            pairs.append((dbg_outs["dVs"], Vs))
        if upto >= 5:
            pairs.append((dbg_outs["dOs"], Os))
        if upto >= 6:
            pairs.append((dbg_outs["dPs"], Ps))
        with nc.Block() as b:
            @b.sync
            def _(s):
                for dst, srcap in pairs:
                    s.dma_start(dst[:], srcap[:]).then_inc(ddone, 16)
                s.wait_ge(ddone, 16 * len(pairs))

    return nc


def kernel(L0_attn_x, L0_attn_w_q, L0_attn_w_k, L0_attn_w_v, L0_attn_w_o,
           L0_ffn_lora_A, L0_ffn_lora_B, L0_ffn_W, L0_ffn_bias,
           L0_ffn_lora_weights):
    xs = np.ascontiguousarray(np.asarray(L0_attn_x, dtype=np.float32))
    wq_ = np.ascontiguousarray(np.asarray(L0_attn_w_q, dtype=np.float32))
    wk_ = np.ascontiguousarray(np.asarray(L0_attn_w_k, dtype=np.float32))
    wv_ = np.ascontiguousarray(np.asarray(L0_attn_w_v, dtype=np.float32))
    wo_ = np.ascontiguousarray(np.asarray(L0_attn_w_o, dtype=np.float32))
    WTm = np.ascontiguousarray(np.asarray(L0_ffn_W, dtype=np.float32).T)
    ATm = np.ascontiguousarray(
        np.asarray(L0_ffn_lora_A, dtype=np.float32).reshape(NR, D).T)
    BSm = np.ascontiguousarray(
        np.asarray(L0_ffn_lora_B, dtype=np.float32).transpose(0, 2, 1).reshape(NR, D))
    bias_ = np.ascontiguousarray(
        np.asarray(L0_ffn_bias, dtype=np.float32).reshape(1, D))
    lwr_ = np.ascontiguousarray(
        np.repeat(np.asarray(L0_ffn_lora_weights, dtype=np.float32), 16
                  ).reshape(NR, 1))
    ident = np.eye(128, dtype=np.float32)

    nc = build_nc()
    common = dict(wq=wq_, wk=wk_, wv=wv_, wo=wo_, WT=WTm, AT=ATm, BS=BSm,
                  bias=bias_, lwr=lwr_, ident=ident)
    in_maps = [dict(x=np.ascontiguousarray(xs[i]), **common) for i in range(8)]
    res = run_bass_kernel_spmd(nc, in_maps, core_ids=list(range(8)))
    out = np.stack([res.results[i]["y"] for i in range(8)], axis=0)
    return out


# version 13
# speedup vs baseline: 1.1316x; 1.0265x over previous
import contextlib
import numpy as np
import concourse.bass as bass
import concourse.mybir as mybir
from concourse.bass_utils import run_bass_kernel_spmd
from concourse.alu_op_type import AluOpType

F32 = mybir.dt.float32
BF16 = mybir.dt.bfloat16
AF = mybir.ActivationFunctionType

S = 2048
D = 4096
C = D // 128
HD = 128
NH = 32
NKV = 8
DKV = NKV * HD
G = S // 512
TS = S // 128
NR = 128
EXP_SHIFT = -8.0
QSCALE = 1.0 / np.sqrt(128.0)


def build_nc(upto=7, dbg=False):
    nc = bass.Bass("TRN2", target_bir_lowering=False)

    x = nc.dram_tensor("x", [S, D], F32, kind="ExternalInput").ap()
    wq = nc.dram_tensor("wq", [D, D], F32, kind="ExternalInput").ap()
    wk = nc.dram_tensor("wk", [D, DKV], F32, kind="ExternalInput").ap()
    wv = nc.dram_tensor("wv", [D, DKV], F32, kind="ExternalInput").ap()
    wo = nc.dram_tensor("wo", [D, D], F32, kind="ExternalInput").ap()
    WT = nc.dram_tensor("WT", [D, D], F32, kind="ExternalInput").ap()
    AT = nc.dram_tensor("AT", [D, NR], F32, kind="ExternalInput").ap()
    BS = nc.dram_tensor("BS", [NR, D], F32, kind="ExternalInput").ap()
    bias = nc.dram_tensor("bias", [1, D], F32, kind="ExternalInput").ap()
    lwr = nc.dram_tensor("lwr", [NR, 1], F32, kind="ExternalInput").ap()
    ident = nc.dram_tensor("ident", [128, 128], F32, kind="ExternalInput").ap()

    y = nc.dram_tensor("y", [S, D], F32, kind="ExternalOutput").ap()

    Qs = nc.dram_tensor("Qs", [NH, 128, S], BF16, kind="Internal").ap()
    Ks = nc.dram_tensor("Ks", [NKV, 128, S], BF16, kind="Internal").ap()
    Vs = nc.dram_tensor("Vs", [TS, 128, DKV], BF16, kind="Internal").ap()
    Os = nc.dram_tensor("Os", [C, 128, S], BF16, kind="Internal").ap()
    XNs = nc.dram_tensor("XNs", [128, C * S], BF16, kind="Internal").ap()
    Ps = nc.dram_tensor("Ps", [S, D], F32, kind="Internal").ap()

    dbg_outs = {}
    if dbg:
        for nm, shp, dt in [("dQs", [NH, 128, S], BF16), ("dKs", [NKV, 128, S], BF16),
                            ("dVs", [TS, 128, DKV], BF16), ("dXN", [128, C * S], BF16),
                            ("dOs", [C, 128, S], BF16), ("dPs", [S, D], F32)]:
            dbg_outs[nm] = nc.dram_tensor(nm, shp, dt, kind="ExternalOutput").ap()

    sLx = [nc.alloc_semaphore(f"sLx{i}") for i in range(2)]
    sWb = [nc.alloc_semaphore(f"sWb{i}") for i in range(2)]
    sLC = nc.alloc_semaphore("sLC")
    sBig = nc.alloc_semaphore("sBig")
    sSt = [nc.alloc_semaphore(f"sSt{i}") for i in range(2)]
    sQd = [nc.alloc_semaphore(f"sQd{i}") for i in range(2)]
    sOd = [nc.alloc_semaphore(f"sOd{i}") for i in range(2)]
    sR = [nc.alloc_semaphore(f"sR{i}") for i in range(2)]
    sYd = [nc.alloc_semaphore(f"sYd{i}") for i in range(2)]
    sXl = [nc.alloc_semaphore(f"sXl{i}") for i in range(4)]
    sV = nc.alloc_semaphore("sV")
    sA = nc.alloc_semaphore("sA")
    sT = nc.alloc_semaphore("sT")
    sP = nc.alloc_semaphore("sP")

    cnt = {"sV": 0, "sA": 0, "sT": 0, "sP": 0, "sLC": 0, "sBig": 0}
    fills = {"sLx": [0, 0], "sWb": [0, 0], "sSt": [0, 0], "sQd": [0, 0],
             "sOd": [0, 0], "sR": [0, 0], "sYd": [0, 0], "sXl": [0, 0, 0, 0]}

    es = contextlib.ExitStack()
    with es:
        wbuf = es.enter_context(nc.sbuf_tensor("wbuf", [128, 2, 32, 256], BF16))
        idb = es.enter_context(nc.sbuf_tensor("idb", [128, 128], BF16))
        ones = es.enter_context(nc.sbuf_tensor("ones", [128, 128], BF16))
        cst = es.enter_context(nc.sbuf_tensor("cst", [128, 4], F32))

        exn = es.enter_context(contextlib.ExitStack())
        xn = exn.enter_context(nc.sbuf_tensor("xn", [128, C * S], BF16))

        with contextlib.ExitStack() as e0:
            xbuf = e0.enter_context(nc.sbuf_tensor("xbuf", [128, 2, D], F32))
            xnt = e0.enter_context(nc.sbuf_tensor("xnt", [128, D], BF16))
            stats = e0.enter_context(nc.sbuf_tensor("stats", [128, 48], F32))
            mv = e0.enter_context(nc.sbuf_tensor("mv", [128, 2], F32))
            std_ = e0.enter_context(nc.sbuf_tensor("std_", [128, 1], F32))
            rstd = e0.enter_context(nc.sbuf_tensor("rstd", [128, 1], F32))
            bneg = e0.enter_context(nc.sbuf_tensor("bneg", [128, 1], F32))
            ps_t = e0.enter_context(nc.psum_tensor("ps_t", [128, 2, 2048], BF16))

            v0, a0, t0 = cnt["sV"], cnt["sA"], cnt["sT"]
            lc0 = cnt["sLC"]
            _sid = nc.enter_named_scope("P0_ln", False)[0]
            with nc.Block() as b:
                @b.gpsimd
                def _(g):
                    g.dma_start(idb[:], ident[:]).then_inc(sLC, 16)
                    g.wait_ge(sLC, lc0 + 16)
                cnt["sLC"] += 16

                @b.sync
                def _(s):
                    for t in range(TS):
                        if t >= 2:
                            s.wait_ge(sA, a0 + (2 if t == 2 else 4 * (t - 2)))
                        s.dma_start(xbuf[:, t % 2, :],
                                    x[t * 128:(t + 1) * 128, :]
                                    ).then_inc(sLx[t % 2], 16)
                    s.wait_ge(sLx[0], 16 * (TS // 2))
                    s.wait_ge(sLx[1], 16 * (TS // 2))
                for t in range(TS):
                    fills["sLx"][t % 2] += 1

                @b.vector
                def _(v):
                    v.memset(ones[:], 1.0)
                    v.memset(cst[:, 0:1], 1e-5)
                    v.memset(cst[:, 1:2], EXP_SHIFT)
                    v.memset(cst[:, 2:3], QSCALE)
                    v.memset(cst[:, 3:4], -1.0).then_inc(sV, 1)
                    for t in range(TS):
                        v.wait_ge(sLx[t % 2], 16 * (t // 2 + 1))
                        last = None
                        for i in range(8):
                            last = v.bn_stats(stats[:, i * 6:(i + 1) * 6],
                                              xbuf[:, t % 2, i * 512:(i + 1) * 512])
                        last.then_inc(sV, 1)
                        v.wait_ge(sV, v0 + 1 + 4 * t + 1)
                        v.bn_aggr(mv[:], stats[:]).then_inc(sV, 1)
                        v.wait_ge(sA, a0 + (1 if t == 0 else 4 * t - 1))
                        if t >= 1:
                            v.wait_ge(sA, a0 + (2 if t == 1 else 4 * (t - 1)))
                        v.reciprocal(rstd[:], std_[:]).then_inc(sV, 1)
                        v.wait_ge(sV, v0 + 1 + 4 * t + 3)
                        v.tensor_scalar(bneg[:], mv[:, 0:1], rstd[:], -1.0,
                                        AluOpType.mult,
                                        AluOpType.mult).then_inc(sV, 1)
                cnt["sV"] += 1 + 4 * TS

                @b.scalar
                def _(sc):
                    for t in range(TS):
                        sc.wait_ge(sV, v0 + 1 + 4 * t + 2)
                        sc.activation(std_[:], mv[:, 1:2], AF.Sqrt,
                                      bias=cst[:, 0:1]).then_inc(sA, 1)
                        sc.wait_ge(sV, v0 + 1 + 4 * t + 4)
                        sc.wait_ge(sLx[t % 2], 16 * (t // 2 + 1))
                        if t >= 1:
                            sc.wait_ge(sT, t0 + 2 * t)
                        sc.activation(xnt[:], xbuf[:, t % 2, :], AF.Identity,
                                      bias=bneg[:], scale=rstd[:]).then_inc(sA, 1)
                        if t >= 1:
                            for half in range(2):
                                sc.wait_ge(sT, t0 + 2 * (t - 1) + half + 1)
                                dst = xn[:].rearrange("p (c s) -> p c s", c=C)[
                                    :, half * 16:(half + 1) * 16,
                                    (t - 1) * 128: t * 128]
                                src = ps_t[:, half, :].rearrange(
                                    "p (c s) -> p c s", c=16)
                                sc.activation(dst, src, AF.Copy, bias=0.0,
                                              scale=1.0).then_inc(sA, 1)
                    for half in range(2):
                        sc.wait_ge(sT, t0 + 2 * (TS - 1) + half + 1)
                        dst = xn[:].rearrange("p (c s) -> p c s", c=C)[
                            :, half * 16:(half + 1) * 16, (TS - 1) * 128: TS * 128]
                        src = ps_t[:, half, :].rearrange("p (c s) -> p c s", c=16)
                        sc.activation(dst, src, AF.Copy, bias=0.0,
                                      scale=1.0).then_inc(sA, 1)
                cnt["sA"] += 4 * TS

                @b.tensor
                def _(t_):
                    t_.wait_ge(sLC, lc0 + 16)
                    for t in range(TS):
                        t_.wait_ge(sA, a0 + (2 if t == 0 else 4 * t))
                        for half in range(2):
                            if t >= 1:
                                t_.wait_ge(sA, a0 + 4 * t + 1 + half)
                            last = None
                            for cc in range(16):
                                c = half * 16 + cc
                                last = t_.transpose(
                                    ps_t[:, half, cc * 128:(cc + 1) * 128],
                                    xnt[:, c * 128:(c + 1) * 128], idb[:])
                            last.then_inc(sT, 1)
                cnt["sT"] += 2 * TS
            nc.leave_named_scope("P0_ln", _sid, False)

        def proj_heads_phase(w_ap, n_blocks, out_s, tag):
            with contextlib.ExitStack() as e1:
                stage = e1.enter_context(
                    nc.sbuf_tensor(f"stg{tag}", [128, 2, 512], BF16))
                ps_k = e1.enter_context(
                    nc.psum_tensor(f"psk{tag}", [128, 2, 512], F32))
                n_groups = n_blocks * 8
                t0, a0 = cnt["sT"], cnt["sA"]
                wfill0 = list(fills["sWb"])
                sfill0 = list(fills["sSt"])

                _sid = nc.enter_named_scope("P12_" + tag, False)[0]
                with nc.Block() as b:
                    @b.gpsimd
                    def _(g):
                        for blk in range(n_blocks):
                            if blk >= 2:
                                g.wait_ge(sT, t0 + 8 * (blk - 1))
                            src = w_ap[:, blk * 256:(blk + 1) * 256].rearrange(
                                "(c p) m -> p c m", p=128)
                            g.dma_start(wbuf[:, blk % 2, :, :],
                                        src).then_inc(sWb[blk % 2], 16)
                        for sl in range(2):
                            nfill = (n_blocks - sl + 1) // 2
                            g.wait_ge(sWb[sl], 16 * (wfill0[sl] + nfill))
                    for blk in range(n_blocks):
                        fills["sWb"][blk % 2] += 1

                    @b.tensor
                    def _(t_):
                        for blk in range(n_blocks):
                            t_.wait_ge(sWb[blk % 2],
                                       16 * (wfill0[blk % 2] + blk // 2 + 1))
                            for hh in range(2):
                                for gg in range(4):
                                    idx = blk * 8 + hh * 4 + gg
                                    if idx >= 2:
                                        t_.wait_ge(sA, a0 + idx - 1)
                                    last = None
                                    for c in range(C):
                                        last = t_.matmul(
                                            ps_k[:, idx % 2, :],
                                            wbuf[:, blk % 2, c,
                                                 hh * 128:(hh + 1) * 128],
                                            xn[:, c * S + gg * 512:
                                               c * S + (gg + 1) * 512],
                                            start=(c == 0), stop=(c == C - 1))
                                    last.then_inc(sT, 1)
                    cnt["sT"] += n_groups

                    @b.scalar
                    def _(sc):
                        for idx in range(n_groups):
                            sc.wait_ge(sT, t0 + idx + 1)
                            if idx >= 2:
                                sc.wait_ge(sSt[idx % 2],
                                           16 * (sfill0[idx % 2] + (idx - 2) // 2 + 1))
                            sc.activation(stage[:, idx % 2, :],
                                          ps_k[:, idx % 2, :], AF.Copy, bias=0.0,
                                          scale=cst[:, 2:3] if tag == "q" else 1.0
                                          ).then_inc(sA, 1)
                    cnt["sA"] += n_groups

                    @b.sync
                    def _(s):
                        for idx in range(n_groups):
                            blk, hh, gg = idx // 8, (idx // 4) % 2, idx % 4
                            head = blk * 2 + hh
                            s.wait_ge(sA, a0 + idx + 1)
                            s.dma_start(out_s[head, :, gg * 512:(gg + 1) * 512],
                                        stage[:, idx % 2, :]
                                        ).then_inc(sSt[idx % 2], 16)
                        for sl in range(2):
                            nfill = (n_groups - sl + 1) // 2
                            s.wait_ge(sSt[sl], 16 * (sfill0[sl] + nfill))
                    for idx in range(n_groups):
                        fills["sSt"][idx % 2] += 1
                nc.leave_named_scope("P12_" + tag, _sid, False)

        if upto >= 2:
            proj_heads_phase(wk, NKV // 2, Ks, "k")
        if upto >= 3:
            proj_heads_phase(wq, NH // 2, Qs, "q")

        if upto >= 4:
            with contextlib.ExitStack() as e3:
                stage = e3.enter_context(nc.sbuf_tensor("stgv", [128, 2, 256], BF16))
                ps_v = e3.enter_context(nc.psum_tensor("psv", [128, 4, 512], F32))
                n_blocks = 4
                n_groups = n_blocks * TS
                t0, a0, big0 = cnt["sT"], cnt["sA"], cnt["sBig"]
                wfill0 = list(fills["sWb"])
                sfill0 = list(fills["sSt"])

                _sid = nc.enter_named_scope("P3_v", False)[0]
                with nc.Block() as b:
                    @b.gpsimd
                    def _(g):
                        for blk in range(n_blocks):
                            if blk >= 2:
                                g.wait_ge(sT, t0 + TS * (blk - 1))
                            src = wv[:, blk * 256:(blk + 1) * 256].rearrange(
                                "(c p) m -> p c m", p=128)
                            g.dma_start(wbuf[:, blk % 2, :, :],
                                        src).then_inc(sWb[blk % 2], 16)
                        for sl in range(2):
                            g.wait_ge(sWb[sl], 16 * (wfill0[sl] + 2))
                    for blk in range(n_blocks):
                        fills["sWb"][blk % 2] += 1

                    @b.tensor
                    def _(t_):
                        for blk in range(n_blocks):
                            t_.wait_ge(sWb[blk % 2],
                                       16 * (wfill0[blk % 2] + blk // 2 + 1))
                            for ts in range(TS):
                                idx = blk * TS + ts
                                if idx >= 4:
                                    t_.wait_ge(sA, a0 + idx - 3)
                                last = None
                                for c in range(C):
                                    last = t_.matmul(
                                        ps_v[:, idx % 4, 0:256],
                                        xn[:, c * S + ts * 128:
                                           c * S + (ts + 1) * 128],
                                        wbuf[:, blk % 2, c, :],
                                        start=(c == 0), stop=(c == C - 1))
                                last.then_inc(sT, 1)
                    cnt["sT"] += n_groups

                    @b.scalar
                    def _(sc):
                        for idx in range(n_groups):
                            sc.wait_ge(sT, t0 + idx + 1)
                            if idx >= 2:
                                sc.wait_ge(sSt[idx % 2],
                                           16 * (sfill0[idx % 2] + (idx - 2) // 2 + 1))
                            sc.activation(stage[:, idx % 2, :],
                                          ps_v[:, idx % 4, 0:256],
                                          AF.Copy, bias=0.0,
                                          scale=1.0).then_inc(sA, 1)
                    cnt["sA"] += n_groups

                    @b.sync
                    def _(s):
                        s.dma_start(XNs[:], xn[:]).then_inc(sBig, 16)
                        for idx in range(n_groups):
                            blk, ts = idx // TS, idx % TS
                            s.wait_ge(sA, a0 + idx + 1)
                            s.dma_start(Vs[ts, :, blk * 256:(blk + 1) * 256],
                                        stage[:, idx % 2, :]
                                        ).then_inc(sSt[idx % 2], 16)
                        for sl in range(2):
                            s.wait_ge(sSt[sl], 16 * (sfill0[sl] + n_groups // 2))
                        s.wait_ge(sBig, big0 + 16)
                    cnt["sBig"] += 16
                    for idx in range(n_groups):
                        fills["sSt"][idx % 2] += 1
                nc.leave_named_scope("P3_v", _sid, False)
        elif dbg:
            dsp = nc.alloc_semaphore("dsp")
            with nc.Block() as b:
                @b.sync
                def _(s):
                    s.dma_start(XNs[:], xn[:]).then_inc(dsp, 16)
                    s.wait_ge(dsp, 16)

        exn.close()

        if upto >= 5:
            with contextlib.ExitStack() as e4:
                K_res = e4.enter_context(nc.sbuf_tensor("K_res", [128, NKV * S], BF16))
                V_res = e4.enter_context(nc.sbuf_tensor("V_res", [128, TS * DKV], BF16))
                qb = e4.enter_context(nc.sbuf_tensor("qb", [128, 2, 512], BF16))
                pT = e4.enter_context(nc.sbuf_tensor("pT", [128, 3, 16 * 512], BF16))
                linv = e4.enter_context(nc.sbuf_tensor("linv", [128, 2, 512], F32))
                ost = e4.enter_context(nc.sbuf_tensor("ost", [128, 2, 512], BF16))
                acc = e4.enter_context(nc.sbuf_tensor("acc", [128, 2, 1024], BF16))
                acc5 = e4.enter_context(nc.sbuf_tensor("acc5", [128, 2, 512], BF16))
                lntmp = e4.enter_context(nc.sbuf_tensor("lntmp", [128, 512], F32))
                ps_sc = e4.enter_context(nc.psum_tensor("ps_sc", [128, 4, 512], F32))
                ps_l = e4.enter_context(nc.psum_tensor("ps_l", [128, 2, 512], F32))
                ps_o = e4.enter_context(nc.psum_tensor("ps_o", [128, 2, 512], F32))
                NIT = NH * G
                t0, a0, v0, big0 = cnt["sT"], cnt["sA"], cnt["sV"], cnt["sBig"]
                p0 = cnt["sP"]
                sV2 = nc.alloc_semaphore("sV2")

                def exp_pos(i, m):
                    if i == 0:
                        return a0 + m + 1
                    return a0 + (10 * i - 1 + m if m < 2 else 10 * i + m + 1)

                def expinv_pos(k):
                    return a0 + (10 * (k + 1) + 2 if k <= NIT - 2 else 10 * NIT)
                qfill0 = list(fills["sQd"])
                ofill0 = list(fills["sOd"])

                _sid = nc.enter_named_scope("P4_attn", False)[0]
                with nc.Block() as b:
                    @b.sync
                    def _(s):
                        s.dma_start(K_res[:].rearrange("p (h s) -> p h s", h=NKV),
                                    Ks.rearrange("h p s -> p h s")).then_inc(sBig, 16)
                        s.dma_start(V_res[:].rearrange("p (j d) -> p j d", j=TS),
                                    Vs.rearrange("j p d -> p j d")).then_inc(sBig, 16)
                        for i in range(NIT):
                            if i >= 2:
                                ip = i - 2
                                s.wait_ge(sT, t0 + (16 if ip == 0 else 17 * ip + 15))
                            h, gg = i // G, i % G
                            s.dma_start(qb[:, i % 2, :],
                                        Qs[h, :, gg * 512:(gg + 1) * 512]
                                        ).then_inc(sQd[i % 2], 16)
                        for sl in range(2):
                            s.wait_ge(sQd[sl], 16 * (qfill0[sl] + NIT // 2))
                        s.wait_ge(sBig, big0 + 32)
                    cnt["sBig"] += 32
                    for i in range(NIT):
                        fills["sQd"][i % 2] += 1

                    def emit_score(t_, i, j):
                        kvh = (i // G) // 4
                        sidx = 16 * i + j
                        if sidx >= 4:
                            c = sidx - 4
                            t_.wait_ge(sA, exp_pos(c // 16, (c % 16) // 2))
                        t_.matmul(ps_sc[:, sidx % 4, :],
                                  K_res[:, kvh * S + j * 128:
                                        kvh * S + (j + 1) * 128],
                                  qb[:, i % 2, :],
                                  start=True, stop=True).then_inc(sT, 1)

                    def emit_lsum(t_, i):
                        t_.wait_ge(sV2, 8 * (i + 1))
                        if i >= 2:
                            t_.wait_ge(sV, v0 + 2 * (i - 1))
                        t_.matmul(ps_l[:, i % 2, :], ones[:], acc5[:, i % 2, :],
                                  start=True, stop=True)

                    def emit_pv(t_, i, j):
                        kvh = (i // G) // 4
                        ins = t_.matmul(
                            ps_o[:, i % 2, :],
                            V_res[:, j * DKV + kvh * 128:
                                  j * DKV + (kvh + 1) * 128],
                            pT[:, i % 3, j * 512:(j + 1) * 512],
                            start=(j == 0), stop=(j == 15))
                        if j == 15:
                            ins.then_inc(sT, 1)

                    @b.tensor
                    def _(t_):
                        t_.wait_ge(sBig, big0 + 32)
                        for i in range(NIT):
                            t_.wait_ge(sQd[i % 2], 16 * (qfill0[i % 2] + i // 2 + 1))
                            for j in range(16):
                                emit_score(t_, i, j)
                                if i >= 1:
                                    emit_pv(t_, i - 1, j)
                                if j == 5 and i >= 1:
                                    emit_lsum(t_, i - 1)
                        emit_lsum(t_, NIT - 1)
                        for j in range(16):
                            emit_pv(t_, NIT - 1, j)
                    cnt["sT"] += 17 * NIT

                    @b.scalar
                    def _(sc):
                        def ln_expinv(k):
                            if k <= NIT - 2:
                                mark = 23 if k == 0 else 17 * (k + 1) + 6
                            else:
                                mark = 17 * NIT
                            sc.wait_ge(sT, t0 + mark)
                            sc.activation(lntmp[:], ps_l[:, k % 2, :],
                                          AF.Ln).then_inc(sA, 1)
                            sc.wait_ge(sA, expinv_pos(k) - 1)
                            sc.activation(linv[:, k % 2, :], lntmp[:], AF.Exp,
                                          scale=cst[:, 3:4]).then_inc(sA, 1)
                        for i in range(NIT):
                            for m in range(8):
                                if i >= 1 and m == 2:
                                    ln_expinv(i - 1)
                                j = 2 * m
                                sidx = 16 * i + j
                                if i >= 3 and m == 0:
                                    sc.wait_ge(sT, t0 + 17 * i - 18)
                                if i == 0:
                                    sc.wait_ge(sT, t0 + j + 2)
                                elif i == 1:
                                    sc.wait_ge(sT, t0 + 17 + j + 1)
                                else:
                                    sc.wait_ge(sT, t0 + 17 * i + j + 1)
                                sc.activation(pT[:, i % 3, j * 512:(j + 2) * 512],
                                              ps_sc[:, sidx % 4:sidx % 4 + 2, :],
                                              AF.Exp,
                                              bias=cst[:, 1:2]).then_inc(sA, 1)
                        ln_expinv(NIT - 1)
                    cnt["sA"] += 10 * NIT

                    def norm_tt(v, i):
                        lend = 17 * i + 33 if i <= NIT - 2 else 17 * NIT
                        v.wait_ge(sT, t0 + lend)
                        v.wait_ge(sA, expinv_pos(i))
                        if i >= 2:
                            v.wait_ge(sOd[i % 2],
                                      16 * (ofill0[i % 2] + (i - 2) // 2 + 1))
                        v.tensor_tensor(ost[:, i % 2, :], ps_o[:, i % 2, :],
                                        linv[:, i % 2, :],
                                        AluOpType.mult).then_inc(sV, 2)

                    @b.vector
                    def _(v):
                        for i in range(NIT):
                            if i >= 2:
                                v.wait_ge(sT, t0 + 17 * (i - 2) + 33)
                            for k in range(1, 8):
                                v.wait_ge(sA, exp_pos(i, k))
                                if k > 1:
                                    v.wait_ge(sV2, 8 * i + k - 1)
                                if k == 1:
                                    v.tensor_tensor(acc[:, i % 2, :],
                                                    pT[:, i % 3, 0:1024],
                                                    pT[:, i % 3, 1024:2048],
                                                    AluOpType.add).then_inc(sV2, 1)
                                else:
                                    v.tensor_tensor(acc[:, i % 2, :],
                                                    acc[:, i % 2, :],
                                                    pT[:, i % 3, k * 1024:(k + 1) * 1024],
                                                    AluOpType.add).then_inc(sV2, 1)
                            v.wait_ge(sV2, 8 * i + 7)
                            v.tensor_tensor(acc5[:, i % 2, :], acc[:, i % 2, 0:512],
                                            acc[:, i % 2, 512:1024],
                                            AluOpType.add).then_inc(sV2, 1)
                            if i >= 1:
                                norm_tt(v, i - 1)
                        norm_tt(v, NIT - 1)
                    cnt["sV"] += 2 * NIT

                    @b.gpsimd
                    def _(g):
                        for i in range(NIT):
                            h, gg = i // G, i % G
                            g.wait_ge(sV, v0 + 2 * (i + 1))
                            g.dma_start(Os[h, :, gg * 512:(gg + 1) * 512],
                                        ost[:, i % 2, :]).then_inc(sOd[i % 2], 16)
                        for sl in range(2):
                            g.wait_ge(sOd[sl], 16 * (ofill0[sl] + NIT // 2))
                    for i in range(NIT):
                        fills["sOd"][i % 2] += 1
                nc.leave_named_scope("P4_attn", _sid, False)

        if upto >= 6:
            with contextlib.ExitStack() as e5:
                o_res = e5.enter_context(nc.sbuf_tensor("o_res", [128, C * S], BF16))
                pstage = e5.enter_context(nc.sbuf_tensor("pstage", [128, 2, 256], F32))
                xsl = e5.enter_context(nc.sbuf_tensor("xsl", [128, 4, 256], F32))
                ps_p = e5.enter_context(nc.psum_tensor("ps_p", [128, 4, 512], F32))
                n_blocks = 16
                n_groups = n_blocks * TS
                t0, v5b, big0 = cnt["sT"], cnt["sV"], cnt["sBig"]
                wfill0 = list(fills["sWb"])
                sfill0 = list(fills["sSt"])
                xfill0 = list(fills["sXl"])

                _sid = nc.enter_named_scope("P5_wo", False)[0]
                with nc.Block() as b:
                    @b.sync
                    def _(s):
                        s.dma_start(o_res[:].rearrange("p (c s) -> p c s", c=C),
                                    Os.rearrange("c p s -> p c s")).then_inc(sBig, 16)
                        for idx in range(n_groups):
                            blk, ts = idx // TS, idx % TS
                            s.wait_ge(sV, v5b + idx + 1)
                            s.dma_start(Ps[ts * 128:(ts + 1) * 128,
                                           blk * 256:(blk + 1) * 256],
                                        pstage[:, idx % 2, :]
                                        ).then_inc(sSt[idx % 2], 16)
                        for sl in range(2):
                            s.wait_ge(sSt[sl], 16 * (sfill0[sl] + n_groups // 2))
                        s.wait_ge(sBig, big0 + 16)
                    cnt["sBig"] += 16
                    for idx in range(n_groups):
                        fills["sSt"][idx % 2] += 1

                    @b.scalar
                    def _(sc):
                        for idx in range(n_groups):
                            blk, ts = idx // TS, idx % TS
                            if idx >= 4:
                                sc.wait_ge(sV, v5b + idx - 3)
                            sc.dma_start(xsl[:, idx % 4, :],
                                         x[ts * 128:(ts + 1) * 128,
                                           blk * 256:(blk + 1) * 256]
                                         ).then_inc(sXl[idx % 4], 16)
                    for idx in range(n_groups):
                        fills["sXl"][idx % 4] += 1

                    @b.gpsimd
                    def _(g):
                        for blk in range(n_blocks):
                            if blk >= 2:
                                g.wait_ge(sT, t0 + TS * (blk - 1))
                            src = wo[:, blk * 256:(blk + 1) * 256].rearrange(
                                "(c p) m -> p c m", p=128)
                            g.dma_start(wbuf[:, blk % 2, :, :],
                                        src).then_inc(sWb[blk % 2], 16)
                        for sl in range(2):
                            g.wait_ge(sWb[sl], 16 * (wfill0[sl] + n_blocks // 2))
                    for blk in range(n_blocks):
                        fills["sWb"][blk % 2] += 1

                    @b.tensor
                    def _(t_):
                        t_.wait_ge(sBig, big0 + 16)
                        for blk in range(n_blocks):
                            t_.wait_ge(sWb[blk % 2],
                                       16 * (wfill0[blk % 2] + blk // 2 + 1))
                            for ts in range(TS):
                                idx = blk * TS + ts
                                if idx >= 4:
                                    t_.wait_ge(sV, v5b + idx - 3)
                                last = None
                                for c in range(C):
                                    last = t_.matmul(
                                        ps_p[:, idx % 4, 0:256],
                                        o_res[:, c * S + ts * 128:
                                              c * S + (ts + 1) * 128],
                                        wbuf[:, blk % 2, c, :],
                                        start=(c == 0), stop=(c == C - 1))
                                last.then_inc(sT, 1)
                    cnt["sT"] += n_groups

                    @b.vector
                    def _(v):
                        for idx in range(n_groups):
                            v.wait_ge(sT, t0 + idx + 1)
                            v.wait_ge(sXl[idx % 4],
                                      16 * (xfill0[idx % 4] + idx // 4 + 1))
                            if idx >= 2:
                                v.wait_ge(sSt[idx % 2],
                                          16 * (sfill0[idx % 2] + (idx - 2) // 2 + 1))
                            v.tensor_tensor(pstage[:, idx % 2, :],
                                            ps_p[:, idx % 4, 0:256],
                                            xsl[:, idx % 4, :],
                                            AluOpType.add).then_inc(sV, 1)
                    cnt["sV"] += n_groups
                nc.leave_named_scope("P5_wo", _sid, False)

        if upto >= 7:
            with contextlib.ExitStack() as e6:
                xn2 = e6.enter_context(nc.sbuf_tensor("xn2", [128, C * S], BF16))
                ATb = e6.enter_context(nc.sbuf_tensor("ATb", [128, C * 128], BF16))
                BSb = e6.enter_context(nc.sbuf_tensor("BSb", [128, D], BF16))
                bi_b = e6.enter_context(nc.sbuf_tensor("bi_b", [1, D], BF16))
                lw_sb = e6.enter_context(nc.sbuf_tensor("lw_sb", [128, 1], F32))
                h_all = e6.enter_context(nc.sbuf_tensor("h_all", [128, S], BF16))
                rsl = e6.enter_context(nc.sbuf_tensor("rsl", [128, 2, 256], F32))
                yst = e6.enter_context(nc.sbuf_tensor("yst", [128, 2, 256], F32))
                ps_f = e6.enter_context(nc.psum_tensor("ps_f", [128, 4, 512], F32))
                n_blocks = 16
                NID = n_blocks * TS
                t0, a0, v0, p0, lc0 = (cnt["sT"], cnt["sA"], cnt["sV"], cnt["sP"],
                                       cnt["sLC"])
                wfill0 = list(fills["sWb"])
                rfill0 = list(fills["sR"])
                yfill0 = list(fills["sYd"])

                _sid = nc.enter_named_scope("P6_ffn", False)[0]

                def wt_load(g, blk):
                    if blk >= 2:
                        g.wait_ge(sT, t0 + 4 + TS * (blk - 1))
                    src = WT[:, blk * 256:(blk + 1) * 256].rearrange(
                        "(c p) m -> p c m", p=128)
                    g.dma_start(wbuf[:, blk % 2, :, :],
                                src).then_inc(sWb[blk % 2], 16)

                with nc.Block() as b:
                    @b.gpsimd
                    def _(g):
                        g.dma_start(xn2[:], XNs[:]).then_inc(sLC, 16)
                        g.dma_start(ATb[:], AT.rearrange("(c p) m -> p c m", p=128)
                                    ).then_inc(sLC, 16)
                        g.dma_start(BSb[:], BS[:],
                                    max_dma_last_dim=8192).then_inc(sLC, 16)
                        g.dma_start(bi_b[:], bias[:],
                                    max_dma_last_dim=8192).then_inc(sLC, 16)
                        g.dma_start(lw_sb[:], lwr[:]).then_inc(sLC, 16)
                        wt_load(g, 0)
                        wt_load(g, 1)
                    cnt["sLC"] += 80
                    for blk in range(n_blocks):
                        fills["sWb"][blk % 2] += 1

                    @b.tensor
                    def _(t_):
                        t_.wait_ge(sLC, lc0 + 80)
                        for gg in range(4):
                            if gg >= 2:
                                t_.wait_ge(sV, v0 + gg - 1)
                            last = None
                            for c in range(C):
                                last = t_.matmul(
                                    ps_f[:, gg % 2, :],
                                    ATb[:, c * 128:(c + 1) * 128],
                                    xn2[:, c * S + gg * 512: c * S + (gg + 1) * 512],
                                    start=(c == 0), stop=(c == C - 1))
                            last.then_inc(sT, 1)
                        t_.wait_ge(sV, v0 + 4)
                        for blk in range(n_blocks):
                            t_.wait_ge(sWb[blk % 2],
                                       16 * (wfill0[blk % 2] + blk // 2 + 1))
                            for ts in range(TS):
                                idx = blk * TS + ts
                                if idx >= 4:
                                    t_.wait_ge(sV, v0 + idx + 1)
                                for c in range(C):
                                    t_.matmul(ps_f[:, idx % 4, 0:256],
                                              xn2[:, c * S + ts * 128:
                                                  c * S + (ts + 1) * 128],
                                              wbuf[:, blk % 2, c, :],
                                              start=(c == 0), stop=False)
                                t_.matmul(ps_f[:, idx % 4, 0:256],
                                          h_all[:, ts * 128:(ts + 1) * 128],
                                          BSb[:, blk * 256:(blk + 1) * 256],
                                          start=False, stop=False)
                                t_.matmul(ps_f[:, idx % 4, 0:256],
                                          ones[0:1, :],
                                          bi_b[:, blk * 256:(blk + 1) * 256],
                                          start=False, stop=True).then_inc(sT, 1)
                    cnt["sT"] += 4 + NID

                    @b.vector
                    def _(v):
                        for gg in range(4):
                            v.wait_ge(sT, t0 + gg + 1)
                            v.tensor_scalar(h_all[:, gg * 512:(gg + 1) * 512],
                                            ps_f[:, gg % 2, :], lw_sb[:], 2.0,
                                            AluOpType.mult,
                                            AluOpType.mult).then_inc(sV, 1)
                        rfill06 = list(fills["sR"])
                        for idx in range(NID):
                            v.wait_ge(sT, t0 + 4 + idx + 1)
                            v.wait_ge(sR[idx % 2],
                                      16 * (rfill06[idx % 2] + idx // 2 + 1))
                            if idx >= 2:
                                v.wait_ge(sYd[idx % 2],
                                          16 * (yfill0[idx % 2] + (idx - 2) // 2 + 1))
                            v.tensor_tensor(yst[:, idx % 2, :],
                                            ps_f[:, idx % 4, 0:256],
                                            rsl[:, idx % 2, :],
                                            AluOpType.add).then_inc(sV, 1)
                    cnt["sV"] += 4 + NID

                    @b.sync
                    def _(s):
                        for idx in range(NID):
                            blk, ts = idx // TS, idx % TS
                            if idx >= 2:
                                s.wait_ge(sV, v0 + 4 + idx - 1)
                            s.dma_start(rsl[:, idx % 2, :],
                                        Ps[ts * 128:(ts + 1) * 128,
                                           blk * 256:(blk + 1) * 256]
                                        ).then_inc(sR[idx % 2], 16)
                        for sl in range(2):
                            s.wait_ge(sR[sl], 16 * (rfill0[sl] + NID // 2))
                    for idx in range(NID):
                        fills["sR"][idx % 2] += 1

                    @b.gpsimd
                    def _(g):
                        for idx in range(NID):
                            if idx >= 11 and (idx - 11) % TS == 0:
                                blk = (idx - 11) // TS + 2
                                if blk < n_blocks:
                                    g.wait_ge(sV, v0 + 4 + idx)
                                    wt_load(g, blk)
                        for sl in range(2):
                            g.wait_ge(sWb[sl], 16 * (wfill0[sl] + n_blocks // 2))
                        g.wait_ge(sLC, lc0 + 80)

                    @b.scalar
                    def _(sc):
                        for idx in range(NID):
                            blk, ts = idx // TS, idx % TS
                            sc.wait_ge(sV, v0 + 4 + idx + 1)
                            sc.dma_start(y[ts * 128:(ts + 1) * 128,
                                           blk * 256:(blk + 1) * 256],
                                         yst[:, idx % 2, :]
                                         ).then_inc(sYd[idx % 2], 16)
                        for sl in range(2):
                            sc.wait_ge(sYd[sl], 16 * (yfill0[sl] + NID // 2))
                    for idx in range(NID):
                        fills["sYd"][idx % 2] += 1
                nc.leave_named_scope("P6_ffn", _sid, False)

    if dbg:
        ddone = nc.alloc_semaphore("ddone")
        pairs = [(dbg_outs["dXN"], XNs)]
        if upto >= 2:
            pairs.append((dbg_outs["dKs"], Ks))
        if upto >= 3:
            pairs.append((dbg_outs["dQs"], Qs))
        if upto >= 4:
            pairs.append((dbg_outs["dVs"], Vs))
        if upto >= 5:
            pairs.append((dbg_outs["dOs"], Os))
        if upto >= 6:
            pairs.append((dbg_outs["dPs"], Ps))
        with nc.Block() as b:
            @b.sync
            def _(s):
                for dst, srcap in pairs:
                    s.dma_start(dst[:], srcap[:]).then_inc(ddone, 16)
                s.wait_ge(ddone, 16 * len(pairs))

    return nc


def kernel(L0_attn_x, L0_attn_w_q, L0_attn_w_k, L0_attn_w_v, L0_attn_w_o,
           L0_ffn_lora_A, L0_ffn_lora_B, L0_ffn_W, L0_ffn_bias,
           L0_ffn_lora_weights):
    xs = np.ascontiguousarray(np.asarray(L0_attn_x, dtype=np.float32))
    wq_ = np.ascontiguousarray(np.asarray(L0_attn_w_q, dtype=np.float32))
    wk_ = np.ascontiguousarray(np.asarray(L0_attn_w_k, dtype=np.float32))
    wv_ = np.ascontiguousarray(np.asarray(L0_attn_w_v, dtype=np.float32))
    wo_ = np.ascontiguousarray(np.asarray(L0_attn_w_o, dtype=np.float32))
    WTm = np.ascontiguousarray(np.asarray(L0_ffn_W, dtype=np.float32).T)
    ATm = np.ascontiguousarray(
        np.asarray(L0_ffn_lora_A, dtype=np.float32).reshape(NR, D).T)
    BSm = np.ascontiguousarray(
        np.asarray(L0_ffn_lora_B, dtype=np.float32).transpose(0, 2, 1).reshape(NR, D))
    bias_ = np.ascontiguousarray(
        np.asarray(L0_ffn_bias, dtype=np.float32).reshape(1, D))
    lwr_ = np.ascontiguousarray(
        np.repeat(np.asarray(L0_ffn_lora_weights, dtype=np.float32), 16
                  ).reshape(NR, 1))
    ident = np.eye(128, dtype=np.float32)

    nc = build_nc()
    common = dict(wq=wq_, wk=wk_, wv=wv_, wo=wo_, WT=WTm, AT=ATm, BS=BSm,
                  bias=bias_, lwr=lwr_, ident=ident)
    in_maps = [dict(x=np.ascontiguousarray(xs[i]), **common) for i in range(8)]
    res = run_bass_kernel_spmd(nc, in_maps, core_ids=list(range(8)))
    out = np.stack([res.results[i]["y"] for i in range(8)], axis=0)
    return out
